# version 12
# baseline (speedup 1.0000x reference)
import bisect
import numpy as np
import ml_dtypes
import concourse.bass as bass
import concourse.mybir as mybir
from concourse.bass_utils import run_bass_kernel_spmd

F32 = mybir.dt.float32
BF16 = mybir.dt.bfloat16
I32 = mybir.dt.int32
U8 = mybir.dt.uint8
AF = mybir.ActivationFunctionType
ALU = mybir.AluOpType
AX = mybir.AxisListType
NPBF = ml_dtypes.bfloat16

P = 128
S = 2048
D = 1024
H = 3072
NT = S // P
ND = D // P
NH = H // P
NG = NT // 4
SEQ_PER_CORE = 2
N_CORES = 8
INV_SQRT_DK = 1.0 / 32.0
DEQ = 1.0 / 127.0
EPS = 1e-5

BLOCK_METHOD = {"pe": "tensor", "act": "scalar", "dve": "vector", "pool": "gpsimd", "sp": "sync"}


class _Op:
    __slots__ = ("i", "eng", "fn", "deps", "name", "flag", "dma_key", "sem_id", "val", "know")

    def __init__(self, i, eng, fn, deps, name, dma_key):
        self.i, self.eng, self.fn, self.deps, self.name, self.dma_key = i, eng, fn, deps, name, dma_key
        self.flag = False
        self.sem_id = None
        self.val = None
        self.know = None


class Builder:

    def __init__(self, nc):
        self.nc = nc
        self.ops = []
        self.lw = {}
        self.rd = {}
        self.name_keys = {}
        self.alias_deps = {}
        self.coarse_keys = set()

    @staticmethod
    def _key_name(k):
        return k[0] if isinstance(k, tuple) else k

    def op(self, eng, fn, reads=(), writes=(), name="", dma_key=None):
        i = len(self.ops)
        deps = set()
        for r in reads:
            if r in self.lw:
                deps.add(self.lw[r])
        for w in writes:
            if w in self.lw:
                deps.add(self.lw[w])
            for r_ in self.rd.get(w, ()):
                deps.add(r_)
            ad = self.alias_deps.get(self._key_name(w))
            if ad:
                deps |= ad
        deps.discard(i)
        o = _Op(i, eng, fn, deps, name, dma_key)
        self.ops.append(o)
        for r in reads:
            self.rd.setdefault(r, []).append(i)
            self.name_keys.setdefault(self._key_name(r), set()).add(r)
        for w in writes:
            self.lw[w] = i
            self.rd[w] = []
            self.name_keys.setdefault(self._key_name(w), set()).add(w)
        return i

    def dma(self, queue, fn, reads=(), writes=(), key=None, name="", coarse=False):
        if key is None:
            key = "dq_" + queue
        if coarse:
            self.coarse_keys.add(key)
        return self.op(queue, fn, reads, writes, name, dma_key=key)

    def pending_ops_of(self, tensor_name):
        out = set()
        for k in self.name_keys.get(tensor_name, ()):
            if k in self.lw:
                out.add(self.lw[k])
            out.update(self.rd.get(k, ()))
        return out

    def add_alias_deps(self, tensor_name, dep_ops):
        if dep_ops:
            self.alias_deps.setdefault(tensor_name, set()).update(dep_ops)

    def finalize_and_emit(self, final_wait_keys=()):
        nc = self.nc
        ops = self.ops
        for o in ops:
            for d in o.deps:
                p = ops[d]
                if p.dma_key is None and not (p.eng == o.eng and p.eng in ("pe", "sp")):
                    p.flag = True
        eng_sems = {}
        dma_sems = {}
        for o in ops:
            if o.dma_key is not None:
                if o.dma_key not in dma_sems:
                    dma_sems[o.dma_key] = nc.alloc_semaphore(name=("sd_" + str(o.dma_key))[:30])
            elif o.flag and o.eng not in eng_sems:
                eng_sems[o.eng] = nc.alloc_semaphore(name="se_" + o.eng)
        eng_counts = {}
        dma_counts = {}
        for o in ops:
            if o.dma_key is not None:
                c = dma_counts.get(o.dma_key, 0) + 16
                dma_counts[o.dma_key] = c
                o.sem_id = ("d", o.dma_key)
                o.val = c
            elif o.flag:
                c = eng_counts.get(o.eng, 0) + 1
                eng_counts[o.eng] = c
                o.sem_id = ("e", o.eng)
                o.val = c

        key_idx = {}
        for o in ops:
            if o.dma_key in self.coarse_keys:
                key_idx.setdefault(o.dma_key, []).append(o.i)

        def eff_val(p, oi):
            if p.dma_key in self.coarse_keys:
                return 16 * bisect.bisect_left(key_idx[p.dma_key], oi)
            return p.val

        last_on_eng = {}
        for o in ops:
            know = dict(last_on_eng.get(o.eng, {}))
            for d in sorted(o.deps):
                p = ops[d]
                if p.sem_id is None:
                    continue
                v = eff_val(p, o.i)
                if know.get(p.sem_id, 0) < v:
                    know[p.sem_id] = v
                for k, v2 in p.know.items():
                    if know.get(k, 0) < v2:
                        know[k] = v2
            o.know = know
            last_on_eng[o.eng] = know

        ops_by_eng = {}
        for o in ops:
            ops_by_eng.setdefault(o.eng, []).append(o)
        n_waits = [0]

        def emit_for(eng_key):
            def body(eng):
                know = {}
                for o in ops_by_eng[eng_key]:
                    for d in sorted(o.deps):
                        p = ops[d]
                        if p.sem_id is None:
                            continue
                        v = eff_val(p, o.i)
                        if know.get(p.sem_id, 0) >= v:
                            continue
                        sem = dma_sems[p.sem_id[1]] if p.sem_id[0] == "d" else eng_sems[p.sem_id[1]]
                        eng.wait_ge(sem, v)
                        n_waits[0] += 1
                        know[p.sem_id] = v
                        for k, v2 in p.know.items():
                            if know.get(k, 0) < v2:
                                know[k] = v2
                    ins = o.fn(eng)
                    if o.sem_id is not None:
                        ins.then_inc(dma_sems[o.sem_id[1]] if o.sem_id[0] == "d" else eng_sems[o.sem_id[1]],
                                     16 if o.sem_id[0] == "d" else 1)
                    for k, v2 in o.know.items():
                        if know.get(k, 0) < v2:
                            know[k] = v2
                if eng_key == "sp":
                    for k in final_wait_keys:
                        if k in dma_counts and know.get(("d", k), 0) < dma_counts[k]:
                            eng.wait_ge(dma_sems[k], dma_counts[k])
            return body

        with nc.Block() as block:
            keys = set(ops_by_eng) | ({"sp"} if final_wait_keys else set())
            for eng_key in keys:
                ops_by_eng.setdefault(eng_key, [])
                getattr(block, BLOCK_METHOD[eng_key])(emit_for(eng_key))
        return {"n_ops": len(ops), "n_waits": n_waits[0],
                "n_by_eng": {k: len(v) for k, v in ops_by_eng.items()}}


class Arena:

    def __init__(self, nc, b):
        self.nc = nc
        self.b = b
        avail = nc.sbuf_bytes_remaining - 2048
        self.backing = nc.alloc_sbuf_tensor("arena", [P, avail], U8)
        self.base = nc.lookup_mloc(self.backing).addr
        self.size = avail
        self.free_list = [[0, avail, set()]]
        self.live = {}
        self.peak = 0

    def alloc(self, name, shape, dtype, side="bottom"):
        nbytes = int(np.prod(shape[1:])) * mybir.dt.size(dtype)
        nbytes = (nbytes + 31) // 32 * 32
        if side == "top":
            cands = sorted(self.free_list, key=lambda e: -e[0])
        else:
            cands = sorted(self.free_list, key=lambda e: e[1])
        for ent in cands:
            if ent[1] >= nbytes:
                if side == "top":
                    off = (ent[0] + ent[1] - nbytes) // 32 * 32
                    if off < ent[0]:
                        continue
                    ent[1] = off - ent[0]
                else:
                    off = ent[0]
                    ent[0] += nbytes
                    ent[1] -= nbytes
                deps = set(ent[2])
                if ent[1] == 0:
                    self.free_list.remove(ent)
                for nm, (o2, n2) in self.live.items():
                    assert off + nbytes <= o2 or o2 + n2 <= off, \
                        f"arena overlap: {name}@({off},{nbytes}) vs {nm}@({o2},{n2})"
                h = self.nc.alloc_sbuf_tensor_at(name, list(shape), dtype, offset=self.base + off)
                self.live[name] = (off, nbytes)
                self.peak = max(self.peak, max(o + n for o, n in self.live.values()))
                self.b.add_alias_deps(name, deps)
                return h
        raise MemoryError(f"arena: no space for {name} ({nbytes}B/part); "
                          f"free={[(e[0], e[1]) for e in self.free_list]}")

    def free(self, name):
        off, nbytes = self.live.pop(name)
        deps = self.b.pending_ops_of(name)
        self.free_list.append([off, nbytes, deps])
        self.free_list.sort(key=lambda e: e[0])
        merged = [self.free_list[0]]
        for ent in self.free_list[1:]:
            last = merged[-1]
            if last[0] + last[1] == ent[0]:
                last[1] += ent[1]
                last[2] = last[2] | ent[2]
            else:
                merged.append(ent)
        self.free_list = merged


def build_program():
    nc = bass.Bass("TRN2", target_bir_lowering=False)

    x_d = nc.dram_tensor("x", [SEQ_PER_CORE, S, D], F32, kind="ExternalInput")
    wq_d = nc.dram_tensor("wq", [D, D], F32, kind="ExternalInput")
    wk_d = nc.dram_tensor("wk", [D, D], F32, kind="ExternalInput")
    wv_d = nc.dram_tensor("wv", [D, D], F32, kind="ExternalInput")
    wo_d = nc.dram_tensor("wo", [D, D], F32, kind="ExternalInput")
    wg_d = nc.dram_tensor("wg", [H, D], I32, kind="ExternalInput")
    wu_d = nc.dram_tensor("wu", [H, D], I32, kind="ExternalInput")
    wd_d = nc.dram_tensor("wd", [D, H], I32, kind="ExternalInput")
    out_d = nc.dram_tensor("out", [SEQ_PER_CORE, S, D], F32, kind="ExternalOutput")
    y_dram = nc.dram_tensor("ydram", [S, D], F32, kind="Internal")
    xbf_dram = nc.dram_tensor("xbf", [S, D], BF16, kind="Internal")
    ybf_dram = nc.dram_tensor("ybf", [S, D], BF16, kind="Internal")
    wgbf_d = nc.dram_tensor("wgbf", [H, D], BF16, kind="Internal")
    wubf_d = nc.dram_tensor("wubf", [H, D], BF16, kind="Internal")
    wdbf_d = nc.dram_tensor("wdbf", [D, H], BF16, kind="Internal")

    PF = nc.alloc_psum_tensor("PF", [P, 3072], F32)
    PB = nc.alloc_psum_tensor("PB", [P, 2048], BF16)
    assert nc.lookup_mloc(PF).bank == 0 and nc.lookup_mloc(PB).bank == 6

    def pf(col0, width):
        ap = PF[:, col0:col0 + width]
        keys = tuple(f"F{bk}" for bk in range(col0 // 512, (col0 + width - 1) // 512 + 1))
        return ap, keys

    def pbank(half, col0, width):
        ap = PB[:, half * 1024 + col0: half * 1024 + col0 + width]
        return ap, (f"B{6 + half}",)

    b = Builder(nc)
    ar = Arena(nc, b)

    identb = ar.alloc("identb", [P, P], BF16)
    maskt = ar.alloc("maskt", [P, 4, 512], BF16)
    sqbuf = ar.alloc("sqbuf", [P, D], F32)
    nmax = ar.alloc("nmax", [P, 2], F32)
    rs_all = ar.alloc("rs_all", [P, NT], F32)
    ri_all = ar.alloc("ri_all", [P, NT], F32)
    s1 = ar.alloc("s1", [P, 2], F32)
    s2 = ar.alloc("s2", [P, 2], F32)
    negmean = ar.alloc("negmean", [P, 2], F32)
    nmean2 = ar.alloc("nmean2", [P, 2], F32)
    var2 = ar.alloc("var2", [P, 2], F32)
    lnl = ar.alloc("lnl", [P, 2], F32)
    rstd = ar.alloc("rstd", [P, 2], F32)
    epsb = ar.alloc("epsb", [P, 1], F32)
    xs = ar.alloc("xs", [P, 2, D], F32)
    b.op("pool", lambda e: e.memset(epsb[:], EPS), writes=[("epsb",)])
    b.op("pool", lambda e: e.memset(identb[:], 0.0), writes=[("identb",)])
    b.op("pool", lambda e: e.affine_select(out=identb[:], in_=identb[:], compare_op=ALU.not_equal,
                                           fill=1.0, base=0, pattern=[[-1, P]], channel_multiplier=1),
         reads=[("identb",)], writes=[("identb",)])
    for j in range(4):
        b.op("pool", lambda e, j=j: e.memset(maskt[:, j, :], 0.0), writes=[("maskt", j)])
        b.op("pool", lambda e, j=j: e.affine_select(out=maskt[:, j, :], in_=maskt[:, j, :],
                                                    compare_op=ALU.is_ge, fill=-1e30, base=j * P,
                                                    pattern=[[-1, 512]], channel_multiplier=1),
             reads=[("maskt", j)], writes=[("maskt", j)])
    masktT = ar.alloc("masktT", [P, 4, 512], BF16)
    ones128 = ar.alloc("ones128", [P, P], BF16)
    b.op("pool", lambda e: e.memset(ones128[:], 1.0), writes=[("ones128",)])
    for j2 in range(4):
        b.op("pool", lambda e, j2=j2: e.memset(masktT[:, j2, :], 1.0), writes=[("masktT", j2)])
        b.op("pool", lambda e, j2=j2: e.affine_select(out=masktT[:, j2, :], in_=masktT[:, j2, :],
                                                      compare_op=ALU.is_ge, fill=0.0, base=-j2 * P,
                                                      pattern=[[1, 512]], channel_multiplier=-1),
             reads=[("masktT", j2)], writes=[("masktT", j2)])
    nm_bf = ar.alloc("nm_bf", [P, 2], BF16)
    nmax_r = ar.alloc("nmax_r", [P, 2], F32)
    diag_sb = ar.alloc("diag_sb", [P, 4, P], BF16)
    mb_s = ar.alloc("mb_s", [P, 512], BF16)

    def emit_ln(src_ap, src_keys, dst_ap, dst_keys, sl, norm_eng="dve"):
        b.op("dve", lambda e: e.reduce_sum(s1[:, sl:sl + 1], src_ap, axis=AX.X),
             reads=src_keys, writes=[("s1", sl)])
        b.op("dve", lambda e: e.scalar_tensor_tensor(sqbuf[:], src_ap, 1.0, src_ap, ALU.mult, ALU.mult,
                                                     accum_out=s2[:, sl:sl + 1]),
             reads=src_keys, writes=[("sqbuf",), ("s2", sl)])
        b.op("dve", lambda e: e.tensor_scalar_mul(negmean[:, sl:sl + 1], s1[:, sl:sl + 1], -1.0 / D),
             reads=[("s1", sl)], writes=[("negmean", sl)])
        b.op("dve", lambda e: e.tensor_scalar(nmean2[:, sl:sl + 1], negmean[:, sl:sl + 1],
                                               negmean[:, sl:sl + 1], -1.0, ALU.mult, ALU.mult),
             reads=[("negmean", sl)], writes=[("nmean2", sl)])
        b.op("dve", lambda e: e.tensor_scalar(var2[:, sl:sl + 1], s2[:, sl:sl + 1],
                                               1.0 / D, nmean2[:, sl:sl + 1], ALU.mult, ALU.add),
             reads=[("s2", sl), ("nmean2", sl)], writes=[("var2", sl)])
        b.op("act", lambda e: e.activation(lnl[:, sl:sl + 1], var2[:, sl:sl + 1], AF.Ln, bias=epsb[:], scale=1.0),
             reads=[("var2", sl), ("epsb",)], writes=[("lnl", sl)])
        b.op("act", lambda e: e.activation(rstd[:, sl:sl + 1], lnl[:, sl:sl + 1], AF.Exp, scale=-0.5),
             reads=[("lnl", sl)], writes=[("rstd", sl)])
        b.op(norm_eng, lambda e: e.tensor_scalar(dst_ap, src_ap, negmean[:, sl:sl + 1], rstd[:, sl:sl + 1],
                                                 ALU.add, ALU.mult),
             reads=list(src_keys) + [("negmean", sl), ("rstd", sl)], writes=dst_keys)

    def emit_seq(s):
        sfx = f"_s{s}"

        xT = ar.alloc("xT" + sfx, [P, ND, S], BF16)
        wqb = ar.alloc("wqb" + sfx, [P, ND, D], BF16)
        wkb = ar.alloc("wkb" + sfx, [P, ND, D], BF16)

        b.dma("pool", lambda e: e.dma_start(out=xbf_dram.ap(), in_=x_d.ap()[s]),
              writes=[("xbf",)], key="xbfc")
        b.dma("pool", lambda e: e.dma_start(out=wqb[:], in_=wq_d.ap().rearrange("(ec p) k -> p ec k", p=P)),
              writes=[("wqb" + sfx,)], key="wq")
        b.dma("pool", lambda e: e.dma_start(out=wkb[:], in_=wk_d.ap().rearrange("(ec p) k -> p ec k", p=P)),
              writes=[("wkb" + sfx,)], key="wk")
        for dc in range(ND):
            b.dma("sp", lambda e, dc=dc: e.dma_start_transpose(xT[:, dc, :], xbf_dram.ap()[:, dc * P:(dc + 1) * P]),
                  reads=[("xbf",)], writes=[("xT" + sfx, dc)], key="xTt", coarse=True)

        def project_fmaj(wb, wname, dstname, dst):
            for m in range(ND):
                for half in range(2):
                    for ec in range(ND):
                        for n in (2 * half, 2 * half + 1):
                            ap, kk = pf((n % 4) * 512, 512)
                            b.op("pe", lambda e, ap=ap, m=m, ec=ec, n=n, wb=wb:
                                 e.matmul(ap, wb[:, ec, m * P:(m + 1) * P], xT[:, ec, n * 512:(n + 1) * 512],
                                          start=(ec == 0), stop=(ec == ND - 1)),
                                 reads=[(wname,), ("xT" + sfx, ec)], writes=list(kk))
                    for n in (2 * half, 2 * half + 1):
                        ap, kk = pf((n % 4) * 512, 512)
                        b.op("act", lambda e, dst=dst, m=m, n=n, ap=ap: e.copy(dst[:, m, n * 512:(n + 1) * 512], ap),
                             reads=list(kk), writes=[(dstname, m, n)])

        qT = ar.alloc("qT" + sfx, [P, ND, S], BF16)
        kT = ar.alloc("kT" + sfx, [P, ND, S], BF16)
        project_fmaj(wqb, "wqb" + sfx, "qT" + sfx, qT)
        project_fmaj(wkb, "wkb" + sfx, "kT" + sfx, kT)
        ar.free("wqb" + sfx)
        ar.free("wkb" + sfx)

        wvb = ar.alloc("wvb" + sfx, [P, ND, D], BF16)
        b.dma("pool", lambda e: e.dma_start(out=wvb[:], in_=wv_d.ap().rearrange("(ec p) k -> p ec k", p=P)),
              writes=[("wvb" + sfx,)], key="wv")
        vtm = ar.alloc("v" + sfx, [P, NT, D], BF16)
        for t in range(NT):
            for ec in range(ND):
                for half in range(2):
                    ap, kk = pf(((t % 2) * 2 + half) * 512, 512)
                    b.op("pe", lambda e, ap=ap, ec=ec, t=t, half=half:
                         e.matmul(ap, xT[:, ec, t * P:(t + 1) * P], wvb[:, ec, half * 512:(half + 1) * 512],
                                  start=(ec == 0), stop=(ec == ND - 1)),
                         reads=[("wvb" + sfx,), ("xT" + sfx, ec)], writes=list(kk))
            for half in range(2):
                ap, kk = pf(((t % 2) * 2 + half) * 512, 512)
                b.op("act", lambda e, t=t, half=half, ap=ap: e.copy(vtm[:, t, half * 512:(half + 1) * 512], ap),
                     reads=list(kk), writes=[("v" + sfx, t)])
        ar.free("wvb" + sfx)
        ar.free("xT" + sfx)

        aoT = ar.alloc("aoT" + sfx, [P, ND, S], BF16)
        wob = ar.alloc("wob" + sfx, [P, ND, D], BF16)
        b.dma("pool", lambda e: e.dma_start(out=wob[:], in_=wo_d.ap().rearrange("(ec p) k -> p ec k", p=P)),
              writes=[("wob" + sfx,)], key="wo")
        if s == 0:
            b.dma("pool", lambda e: e.dma_start(out=wgbf_d.ap(), in_=wg_d.ap()), writes=[("wgbf",)], key="wgc")
            b.dma("pool", lambda e: e.dma_start(out=wubf_d.ap(), in_=wu_d.ap()), writes=[("wubf",)], key="wuc")
            b.dma("pool", lambda e: e.dma_start(out=wdbf_d.ap(), in_=wd_d.ap()), writes=[("wdbf",)], key="wdc")
        scb = ar.alloc("sc" + sfx, [P, S], F32)
        Eb = ar.alloc("E" + sfx, [P, S], BF16)
        ET = ar.alloc("ET" + sfx, [P, NT, 512], BF16)
        scT_s = ar.alloc("scT" + sfx, [P, 2, 512], F32)
        scT_e = ar.alloc("scTe" + sfx, [P, 2, 512], BF16)

        def emit_scores_softmax(t):
            g = t // 4
            wlen = (g + 1) * 512
            j = t % 4
            sl = t % 2
            for nb in range(g + 1):
                ap, kk = pf(nb * 512, 512)
                for ec in range(ND):
                    b.op("pe", lambda e, ap=ap, ec=ec, t=t, nb=nb:
                         e.matmul(ap, qT[:, ec, t * P:(t + 1) * P], kT[:, ec, nb * 512:(nb + 1) * 512],
                                  start=(ec == 0), stop=(ec == ND - 1)),
                         reads=[("qT" + sfx, ec, t // 4), ("kT" + sfx, ec, nb)], writes=list(kk))
                if nb == g:
                    b.op("dve", lambda e, ap=ap, nb=nb, j=j:
                         e.scalar_tensor_tensor(scb[:, nb * 512:(nb + 1) * 512], ap, INV_SQRT_DK,
                                                maskt[:, j, :], ALU.mult, ALU.add),
                         reads=list(kk) + [("maskt", j)], writes=[("sc" + sfx, nb)])
                else:
                    b.op("act", lambda e, ap=ap, nb=nb:
                         e.activation(scb[:, nb * 512:(nb + 1) * 512], ap, AF.Copy, scale=INV_SQRT_DK),
                         reads=list(kk), writes=[("sc" + sfx, nb)])
            sckeys = [("sc" + sfx, nb) for nb in range(g + 1)]
            b.op("dve", lambda e, wlen=wlen, sl=sl:
                 e.reduce_max(nmax[:, sl:sl + 1], scb[:, 0:wlen], axis=AX.X, negate=True),
                 reads=sckeys, writes=[("nmax", sl)])
            b.op("dve", lambda e, sl=sl: e.tensor_copy(nm_bf[:, sl:sl + 1], nmax[:, sl:sl + 1]),
                 reads=[("nmax", sl)], writes=[("nmbf", sl)])
            b.op("dve", lambda e, sl=sl: e.tensor_copy(nmax_r[:, sl:sl + 1], nm_bf[:, sl:sl + 1]),
                 reads=[("nmbf", sl)], writes=[("nmr", sl)])
            b.op("act", lambda e, wlen=wlen, sl=sl, t=t:
                 e.activation(Eb[:, 0:wlen], scb[:, 0:wlen], AF.Exp, bias=nmax_r[:, sl:sl + 1], scale=1.0,
                              accum_out=rs_all[:, t:t + 1]),
                 reads=sckeys + [("nmr", sl)], writes=[("E" + sfx,), ("rs", t)])
            b.op("dve", lambda e, sl=sl, j=j: e.tensor_scalar_mul(diag_sb[:, j, :], identb[:], nmax_r[:, sl:sl + 1]),
                 reads=[("identb",), ("nmr", sl)], writes=[("diag", j)])

        def emit_scoresT(g):
            ncs = 4 * (g + 1)
            for j in range(4):
                b.op("pe", lambda e, j=j: e.matmul(PF[:, 4 * 512 + j * P:4 * 512 + (j + 1) * P],
                                                   ones128[:], diag_sb[:, j, :], start=True, stop=True),
                     reads=[("ones128",), ("diag", j)], writes=["F4"])
            apm, kkm = pf(4 * 512, 512)
            b.op("dve", lambda e, apm=apm: e.tensor_copy(mb_s[:], apm),
                 reads=list(kkm), writes=[("mbs",)])
            for cs in range(ncs):
                bank = 4 + cs % 2
                ap, kk = pf(bank * 512, 512)
                for ec in range(ND):
                    b.op("pe", lambda e, ap=ap, ec=ec, cs=cs, g=g:
                         e.matmul(ap, kT[:, ec, cs * P:(cs + 1) * P], qT[:, ec, g * 512:(g + 1) * 512],
                                  start=(ec == 0), stop=(ec == ND - 1)),
                         reads=[("kT" + sfx, ec, cs // 4), ("qT" + sfx, ec, g)], writes=list(kk))
                slot = cs % 2
                b.op("dve", lambda e, ap=ap, slot=slot:
                     e.scalar_tensor_tensor(scT_s[:, slot, :], ap, INV_SQRT_DK, mb_s[:], ALU.mult, ALU.add),
                     reads=list(kk) + [("mbs",)], writes=[("scT" + sfx, slot)])
                if cs >= 4 * g:
                    j2 = cs - 4 * g
                    b.op("act", lambda e, slot=slot: e.activation(scT_e[:, slot, :], scT_s[:, slot, :], AF.Exp),
                         reads=[("scT" + sfx, slot)], writes=[("scTe" + sfx, slot)])
                    b.op("dve", lambda e, cs=cs, slot=slot, j2=j2:
                         e.tensor_tensor(ET[:, cs, :], scT_e[:, slot, :], masktT[:, j2, :], ALU.mult),
                         reads=[("scTe" + sfx, slot), ("masktT", j2)], writes=[("ETc" + sfx, cs)])
                else:
                    b.op("act", lambda e, cs=cs, slot=slot: e.activation(ET[:, cs, :], scT_s[:, slot, :], AF.Exp),
                         reads=[("scT" + sfx, slot)], writes=[("ETc" + sfx, cs)])

        def emit_AV(g):
            ncs = 4 * (g + 1)
            for vc in range(ND):
                ap, kk = pf((4 + vc % 2) * 512, 512)
                for cs in range(ncs):
                    b.op("pe", lambda e, ap=ap, cs=cs, vc=vc, ncs=ncs:
                         e.matmul(ap, vtm[:, cs, vc * P:(vc + 1) * P], ET[:, cs, :],
                                  start=(cs == 0), stop=(cs == ncs - 1)),
                         reads=[("v" + sfx, cs), ("ETc" + sfx, cs)],
                         writes=list(kk))
                b.op("dve", lambda e, ap=ap, vc=vc, g=g: e.tensor_copy(aoT[:, vc, g * 512:(g + 1) * 512], ap),
                     reads=list(kk), writes=[("aoT" + sfx, vc, g)])

        for g in range(NG):
            for j in range(4):
                emit_scores_softmax(4 * g + j)
            emit_scoresT(g)
            emit_AV(g)
        b.op("dve", lambda e: e.reciprocal(ri_all[:], rs_all[:]),
             reads=[("rs", t) for t in range(NT)], writes=[("ri",)])

        ar.free("qT" + sfx)
        ar.free("kT" + sfx)
        ar.free("v" + sfx)
        ar.free("sc" + sfx)
        ar.free("E" + sfx)
        ar.free("ET" + sfx)
        ar.free("scT" + sfx)
        ar.free("scTe" + sfx)

        r1 = ar.alloc("r1" + sfx, [P, 2, D], F32)
        ybuf = ar.alloc("y" + sfx, [P, 2, D], F32)
        ybb = ar.alloc("ybb" + sfx, [P, 2, D], BF16)
        yT = ar.alloc("yT" + sfx, [P, ND, S], BF16, side="top")

        wgTb = ar.alloc("wgTb" + sfx, [P, 2, ND, 512], BF16, side="top")
        wuTb = ar.alloc("wuTb" + sfx, [P, 2, ND, 512], BF16, side="top")

        def emit_wtransposes(hgrp):
            slw = hgrp % 2
            for dc in range(ND):
                b.dma("sp", lambda e, dc=dc, hgrp=hgrp, slw=slw:
                      e.dma_start_transpose(wgTb[:, slw, dc, :],
                                            wgbf_d.ap()[hgrp * 512:(hgrp + 1) * 512, dc * P:(dc + 1) * P]),
                      reads=[("wgbf",)], writes=[("wgTb" + sfx, slw)], key=f"wgt{slw}", coarse=True)
                b.dma("sp", lambda e, dc=dc, hgrp=hgrp, slw=slw:
                      e.dma_start_transpose(wuTb[:, slw, dc, :],
                                            wubf_d.ap()[hgrp * 512:(hgrp + 1) * 512, dc * P:(dc + 1) * P]),
                      reads=[("wubf",)], writes=[("wuTb" + sfx, slw)], key=f"wut{slw}", coarse=True)

        emit_wtransposes(0)
        emit_wtransposes(1)

        for t in range(NT):
            sl = t % 2
            bz = 0 if t % 2 == 0 else 2
            b.dma("act", lambda e, t=t, sl=sl: e.dma_start(out=xs[:, sl, :], in_=x_d.ap()[s, t * P:(t + 1) * P, :]),
                  writes=[("xs", sl)], key=f"xs{sl}")
            for half in range(2):
                for vc in range(ND):
                    ap, kk = pf((bz + half) * 512, 512)
                    b.op("pe", lambda e, ap=ap, vc=vc, t=t, half=half:
                         e.matmul(ap, aoT[:, vc, t * P:(t + 1) * P], wob[:, vc, half * 512:(half + 1) * 512],
                                  start=(vc == 0), stop=(vc == ND - 1)),
                         reads=[("wob" + sfx,), ("aoT" + sfx, vc, t // 4)], writes=list(kk))
            apz, kkz = pf(bz * 512, 1024)
            b.op("dve", lambda e, apz=apz, sl=sl, t=t:
                 e.scalar_tensor_tensor(r1[:, sl, :], apz, ri_all[:, t:t + 1], xs[:, sl, :], ALU.mult, ALU.add),
                 reads=list(kkz) + [("xs", sl), ("ri",)], writes=[("r1" + sfx, sl)])
            emit_ln(r1[:, sl, :], [("r1" + sfx, sl)], ybuf[:, sl, :], [("y" + sfx, sl)], sl)
            b.op("pool", lambda e, sl=sl: e.tensor_copy(ybb[:, sl, :], ybuf[:, sl, :]),
                 reads=[("y" + sfx, sl)], writes=[("ybb" + sfx, sl)])
            b.dma("act", lambda e, t=t, sl=sl: e.dma_start(out=y_dram.ap()[t * P:(t + 1) * P, :], in_=ybuf[:, sl, :]),
                  reads=[("y" + sfx, sl)], writes=[("ydram", t)], key=f"yst{sl}")
            b.dma("act", lambda e, t=t, sl=sl: e.dma_start(out=ybf_dram.ap()[t * P:(t + 1) * P, :], in_=ybb[:, sl, :]),
                  reads=[("ybb" + sfx, sl)], writes=[("ybf", t)], key=f"ybst{sl}")
            if t % 8 == 7:
                half_seq = t // 8
                for dc in range(ND):
                    b.dma("sp", lambda e, dc=dc, hs=half_seq:
                          e.dma_start_transpose(yT[:, dc, hs * 1024:(hs + 1) * 1024],
                                                ybf_dram.ap()[hs * 1024:(hs + 1) * 1024, dc * P:(dc + 1) * P]),
                          reads=[("ybf", tt) for tt in range(half_seq * 8, half_seq * 8 + 8)],
                          writes=[("yT" + sfx, dc, half_seq)], key=f"yTt{half_seq}", coarse=True)

        ar.free("aoT" + sfx)
        ar.free("wob" + sfx)
        ar.free("r1" + sfx)
        ar.free("y" + sfx)
        ar.free("ybb" + sfx)

        h_actT = ar.alloc("h" + sfx, [P, NH, S], BF16)
        sg = ar.alloc("sg" + sfx, [P, 2, 512], BF16, side="top")
        wdTa = ar.alloc("wdTa" + sfx, [P, 10, D], BF16, side="top")
        for hc in range(10):
            b.dma("sp", lambda e, hc=hc: e.dma_start_transpose(wdTa[:, hc, :], wdbf_d.ap()[:, hc * P:(hc + 1) * P]),
                  reads=[("wdbf",)], writes=[("wdTa" + sfx, hc)], key="wdta", coarse=True)

        for hgrp in range(NH // 4):
            slw = hgrp % 2
            for hl in range(4):
                hc = hgrp * 4 + hl
                for npair in range(2):
                    for dc in range(ND):
                        for i in (0, 1):
                            n = 2 * npair + i
                            apg, kkg = pf((i) * 512, 512)
                            b.op("pe", lambda e, apg=apg, dc=dc, n=n, slw=slw, hl=hl:
                                 e.matmul(apg, wgTb[:, slw, dc, hl * P:(hl + 1) * P],
                                          yT[:, dc, n * 512:(n + 1) * 512],
                                          start=(dc == 0), stop=(dc == ND - 1)),
                                 reads=[("wgTb" + sfx, slw), ("yT" + sfx, dc, n // 2)],
                                 writes=list(kkg))
                    for dc in range(ND):
                        for i in (0, 1):
                            n = 2 * npair + i
                            apu, kku = pf((2 + i) * 512, 512)
                            b.op("pe", lambda e, apu=apu, dc=dc, n=n, slw=slw, hl=hl:
                                 e.matmul(apu, wuTb[:, slw, dc, hl * P:(hl + 1) * P],
                                          yT[:, dc, n * 512:(n + 1) * 512],
                                          start=(dc == 0), stop=(dc == ND - 1)),
                                 reads=[("wuTb" + sfx, slw), ("yT" + sfx, dc, n // 2)],
                                 writes=list(kku))
                    for i in (0, 1):
                        n = 2 * npair + i
                        apg, kkg = pf(i * 512, 512)
                        apu, kku = pf((2 + i) * 512, 512)
                        b.op("act", lambda e, apg=apg, n=n: e.activation(sg[:, n % 2, :], apg, AF.Silu, scale=DEQ),
                             reads=list(kkg), writes=[("sg" + sfx, n % 2)])
                        b.op("dve", lambda e, apu=apu, n=n, hc=hc:
                             e.tensor_tensor(h_actT[:, hc, n * 512:(n + 1) * 512], apu, sg[:, n % 2, :], ALU.mult),
                             reads=list(kku) + [("sg" + sfx, n % 2)], writes=[("h" + sfx, hc, n)])
            if hgrp + 2 < NH // 4:
                emit_wtransposes(hgrp + 2)

        ar.free("yT" + sfx)
        ar.free("wgTb" + sfx)
        ar.free("wuTb" + sfx)
        ar.free("sg" + sfx)

        wdTb_t = ar.alloc("wdTb" + sfx, [P, NH - 10, D], BF16, side="top")
        ys = ar.alloc("ys" + sfx, [P, 2, D], F32, side="top")
        r2 = ar.alloc("r2" + sfx, [P, 2, D], F32, side="top")
        outb = ar.alloc("outb" + sfx, [P, 2, D], F32, side="top")

        for hc in range(10, NH):
            b.dma("sp", lambda e, hc=hc:
                  e.dma_start_transpose(wdTb_t[:, hc - 10, :], wdbf_d.ap()[:, hc * P:(hc + 1) * P]),
                  reads=[("wdbf",)], writes=[("wdTb" + sfx, hc - 10)], key="wdtb", coarse=True)

        def wdt_ap(hc, half):
            if hc < 10:
                return wdTa[:, hc, half * 512:(half + 1) * 512], ("wdTa" + sfx, hc)
            return wdTb_t[:, hc - 10, half * 512:(half + 1) * 512], ("wdTb" + sfx, hc - 10)

        for t in range(NT):
            sl = t % 2
            bz = 0 if t % 2 == 0 else 2
            b.dma("act", lambda e, t=t, sl=sl: e.dma_start(out=ys[:, sl, :], in_=y_dram.ap()[t * P:(t + 1) * P, :]),
                  reads=[("ydram", t)], writes=[("ys" + sfx, sl)], key=f"ys{sl}")
            for hc in range(NH):
                for half in range(2):
                    ap, kk = pf((bz + half) * 512, 512)
                    wap, wkey = wdt_ap(hc, half)
                    b.op("pe", lambda e, ap=ap, hc=hc, t=t, wap=wap:
                         e.matmul(ap, h_actT[:, hc, t * P:(t + 1) * P], wap,
                                  start=(hc == 0), stop=(hc == NH - 1)),
                         reads=[("h" + sfx, hc, t // 4), wkey], writes=list(kk))
            apz, kkz = pf(bz * 512, 1024)
            b.op("dve", lambda e, apz=apz, sl=sl:
                 e.scalar_tensor_tensor(r2[:, sl, :], apz, DEQ * DEQ, ys[:, sl, :], ALU.mult, ALU.add),
                 reads=list(kkz) + [("ys" + sfx, sl)], writes=[("r2" + sfx, sl)])
            emit_ln(r2[:, sl, :], [("r2" + sfx, sl)], outb[:, sl, :], [("outb" + sfx, sl)], sl)
            b.dma("act", lambda e, t=t, sl=sl: e.dma_start(out=out_d.ap()[s, t * P:(t + 1) * P, :], in_=outb[:, sl, :]),
                  reads=[("outb" + sfx, sl)], key=f"ost{sl}")

        ar.free("wdTa" + sfx)
        ar.free("wdTb" + sfx)
        ar.free("h" + sfx)
        ar.free("ys" + sfx)
        ar.free("r2" + sfx)
        ar.free("outb" + sfx)

    for s in range(SEQ_PER_CORE):
        emit_seq(s)

    stats = b.finalize_and_emit(final_wait_keys=["ost0", "ost1"])
    stats["sbuf_peak"] = ar.peak
    return nc, stats


def kernel(L0_attn_x, L0_attn_Wq, L0_attn_Wk, L0_attn_Wv, L0_attn_Wo,
           L0_ffn_w_gate, L0_ffn_w_up, L0_ffn_w_down):
    x = np.ascontiguousarray(np.asarray(L0_attn_x, dtype=np.float32))
    wq = np.ascontiguousarray(np.asarray(L0_attn_Wq, dtype=np.float32))
    wk = np.ascontiguousarray(np.asarray(L0_attn_Wk, dtype=np.float32))
    wv = np.ascontiguousarray(np.asarray(L0_attn_Wv, dtype=np.float32))
    wo = np.ascontiguousarray(np.asarray(L0_attn_Wo, dtype=np.float32))
    wg = np.ascontiguousarray(np.asarray(L0_ffn_w_gate, dtype=np.int32))
    wu = np.ascontiguousarray(np.asarray(L0_ffn_w_up, dtype=np.int32))
    wd = np.ascontiguousarray(np.asarray(L0_ffn_w_down, dtype=np.int32))

    nc, stats = build_program()
    in_maps = []
    for c in range(N_CORES):
        in_maps.append({
            "x": x[c * SEQ_PER_CORE:(c + 1) * SEQ_PER_CORE],
            "wq": wq, "wk": wk, "wv": wv, "wo": wo,
            "wg": wg, "wu": wu, "wd": wd,
        })
    res = run_bass_kernel_spmd(nc, in_maps, core_ids=list(range(N_CORES)))
    out = np.concatenate([r["out"] for r in res.results], axis=0)
    return out.astype(np.float32)


if __name__ == "__main__":
    nc, stats = build_program()
    print("build ok:", stats)


# version 13
# speedup vs baseline: 1.1158x; 1.1158x over previous
import bisect
import numpy as np
import ml_dtypes
import concourse.bass as bass
import concourse.mybir as mybir
from concourse.bass_utils import run_bass_kernel_spmd

F32 = mybir.dt.float32
BF16 = mybir.dt.bfloat16
I32 = mybir.dt.int32
U8 = mybir.dt.uint8
AF = mybir.ActivationFunctionType
ALU = mybir.AluOpType
AX = mybir.AxisListType
NPBF = ml_dtypes.bfloat16

P = 128
S = 2048
D = 1024
H = 3072
NT = S // P
ND = D // P
NH = H // P
NG = NT // 4
SEQ_PER_CORE = 2
N_CORES = 8
INV_SQRT_DK = 1.0 / 32.0
DEQ = 1.0 / 127.0
EPS = 1e-5

BLOCK_METHOD = {"pe": "tensor", "act": "scalar", "dve": "vector", "pool": "gpsimd", "sp": "sync"}


class _Op:
    __slots__ = ("i", "eng", "fn", "deps", "name", "flag", "dma_key", "sem_id", "val", "know")

    def __init__(self, i, eng, fn, deps, name, dma_key):
        self.i, self.eng, self.fn, self.deps, self.name, self.dma_key = i, eng, fn, deps, name, dma_key
        self.flag = False
        self.sem_id = None
        self.val = None
        self.know = None


class Builder:

    def __init__(self, nc):
        self.nc = nc
        self.ops = []
        self.lw = {}
        self.rd = {}
        self.name_keys = {}
        self.alias_deps = {}
        self.coarse_keys = set()

    @staticmethod
    def _key_name(k):
        return k[0] if isinstance(k, tuple) else k

    def op(self, eng, fn, reads=(), writes=(), name="", dma_key=None):
        i = len(self.ops)
        deps = set()
        for r in reads:
            if r in self.lw:
                deps.add(self.lw[r])
        for w in writes:
            if w in self.lw:
                deps.add(self.lw[w])
            for r_ in self.rd.get(w, ()):
                deps.add(r_)
            ad = self.alias_deps.get(self._key_name(w))
            if ad:
                deps |= ad
        deps.discard(i)
        o = _Op(i, eng, fn, deps, name, dma_key)
        self.ops.append(o)
        for r in reads:
            self.rd.setdefault(r, []).append(i)
            self.name_keys.setdefault(self._key_name(r), set()).add(r)
        for w in writes:
            self.lw[w] = i
            self.rd[w] = []
            self.name_keys.setdefault(self._key_name(w), set()).add(w)
        return i

    def dma(self, queue, fn, reads=(), writes=(), key=None, name="", coarse=False):
        if key is None:
            key = "dq_" + queue
        if coarse:
            self.coarse_keys.add(key)
        return self.op(queue, fn, reads, writes, name, dma_key=key)

    def pending_ops_of(self, tensor_name):
        out = set()
        for k in self.name_keys.get(tensor_name, ()):
            if k in self.lw:
                out.add(self.lw[k])
            out.update(self.rd.get(k, ()))
        return out

    def add_alias_deps(self, tensor_name, dep_ops):
        if dep_ops:
            self.alias_deps.setdefault(tensor_name, set()).update(dep_ops)

    def finalize_and_emit(self, final_wait_keys=()):
        nc = self.nc
        ops = self.ops
        for o in ops:
            for d in o.deps:
                p = ops[d]
                if p.dma_key is None and not (p.eng == o.eng and p.eng in ("pe", "sp")):
                    p.flag = True
        eng_sems = {}
        dma_sems = {}
        for o in ops:
            if o.dma_key is not None:
                if o.dma_key not in dma_sems:
                    dma_sems[o.dma_key] = nc.alloc_semaphore(name=("sd_" + str(o.dma_key))[:30])
            elif o.flag and o.eng not in eng_sems:
                eng_sems[o.eng] = nc.alloc_semaphore(name="se_" + o.eng)
        eng_counts = {}
        dma_counts = {}
        for o in ops:
            if o.dma_key is not None:
                c = dma_counts.get(o.dma_key, 0) + 16
                dma_counts[o.dma_key] = c
                o.sem_id = ("d", o.dma_key)
                o.val = c
            elif o.flag:
                c = eng_counts.get(o.eng, 0) + 1
                eng_counts[o.eng] = c
                o.sem_id = ("e", o.eng)
                o.val = c

        key_idx = {}
        for o in ops:
            if o.dma_key in self.coarse_keys:
                key_idx.setdefault(o.dma_key, []).append(o.i)

        def eff_val(p, oi):
            if p.dma_key in self.coarse_keys:
                return 16 * bisect.bisect_left(key_idx[p.dma_key], oi)
            return p.val

        last_on_eng = {}
        for o in ops:
            know = dict(last_on_eng.get(o.eng, {}))
            for d in sorted(o.deps):
                p = ops[d]
                if p.sem_id is None:
                    continue
                v = eff_val(p, o.i)
                if know.get(p.sem_id, 0) < v:
                    know[p.sem_id] = v
                for k, v2 in p.know.items():
                    if know.get(k, 0) < v2:
                        know[k] = v2
            o.know = know
            last_on_eng[o.eng] = know

        ops_by_eng = {}
        for o in ops:
            ops_by_eng.setdefault(o.eng, []).append(o)
        n_waits = [0]

        def emit_for(eng_key):
            def body(eng):
                know = {}
                for o in ops_by_eng[eng_key]:
                    for d in sorted(o.deps):
                        p = ops[d]
                        if p.sem_id is None:
                            continue
                        v = eff_val(p, o.i)
                        if know.get(p.sem_id, 0) >= v:
                            continue
                        sem = dma_sems[p.sem_id[1]] if p.sem_id[0] == "d" else eng_sems[p.sem_id[1]]
                        eng.wait_ge(sem, v)
                        n_waits[0] += 1
                        know[p.sem_id] = v
                        for k, v2 in p.know.items():
                            if know.get(k, 0) < v2:
                                know[k] = v2
                    ins = o.fn(eng)
                    if o.sem_id is not None:
                        ins.then_inc(dma_sems[o.sem_id[1]] if o.sem_id[0] == "d" else eng_sems[o.sem_id[1]],
                                     16 if o.sem_id[0] == "d" else 1)
                    for k, v2 in o.know.items():
                        if know.get(k, 0) < v2:
                            know[k] = v2
                if eng_key == "sp":
                    for k in final_wait_keys:
                        if k in dma_counts and know.get(("d", k), 0) < dma_counts[k]:
                            eng.wait_ge(dma_sems[k], dma_counts[k])
            return body

        with nc.Block() as block:
            keys = set(ops_by_eng) | ({"sp"} if final_wait_keys else set())
            for eng_key in keys:
                ops_by_eng.setdefault(eng_key, [])
                getattr(block, BLOCK_METHOD[eng_key])(emit_for(eng_key))
        return {"n_ops": len(ops), "n_waits": n_waits[0],
                "n_by_eng": {k: len(v) for k, v in ops_by_eng.items()}}


class Arena:

    def __init__(self, nc, b):
        self.nc = nc
        self.b = b
        avail = nc.sbuf_bytes_remaining - 2048
        self.backing = nc.alloc_sbuf_tensor("arena", [P, avail], U8)
        self.base = nc.lookup_mloc(self.backing).addr
        self.size = avail
        self.free_list = [[0, avail, set()]]
        self.live = {}
        self.peak = 0

    def alloc(self, name, shape, dtype):
        nbytes = int(np.prod(shape[1:])) * mybir.dt.size(dtype)
        nbytes = (nbytes + 31) // 32 * 32
        for ent in sorted(self.free_list, key=lambda e: e[1]):
            if ent[1] >= nbytes:
                off = ent[0]
                deps = set(ent[2])
                ent[0] += nbytes
                ent[1] -= nbytes
                if ent[1] == 0:
                    self.free_list.remove(ent)
                for nm, (o2, n2) in self.live.items():
                    assert off + nbytes <= o2 or o2 + n2 <= off, \
                        f"arena overlap: {name}@({off},{nbytes}) vs {nm}@({o2},{n2})"
                h = self.nc.alloc_sbuf_tensor_at(name, list(shape), dtype, offset=self.base + off)
                self.live[name] = (off, nbytes)
                self.peak = max(self.peak, max(o + n for o, n in self.live.values()))
                self.b.add_alias_deps(name, deps)
                return h
        raise MemoryError(f"arena: no space for {name} ({nbytes}B/part); "
                          f"free={[(e[0], e[1]) for e in self.free_list]}")

    def free(self, name):
        off, nbytes = self.live.pop(name)
        deps = self.b.pending_ops_of(name)
        self.free_list.append([off, nbytes, deps])
        self.free_list.sort(key=lambda e: e[0])
        merged = [self.free_list[0]]
        for ent in self.free_list[1:]:
            last = merged[-1]
            if last[0] + last[1] == ent[0]:
                last[1] += ent[1]
                last[2] = last[2] | ent[2]
            else:
                merged.append(ent)
        self.free_list = merged


def build_program():
    nc = bass.Bass("TRN2", target_bir_lowering=False)

    x_d = nc.dram_tensor("x", [SEQ_PER_CORE, S, D], F32, kind="ExternalInput")
    wq_d = nc.dram_tensor("wq", [D, D], F32, kind="ExternalInput")
    wk_d = nc.dram_tensor("wk", [D, D], F32, kind="ExternalInput")
    wv_d = nc.dram_tensor("wv", [D, D], F32, kind="ExternalInput")
    wo_d = nc.dram_tensor("wo", [D, D], F32, kind="ExternalInput")
    wg_d = nc.dram_tensor("wg", [H, D], I32, kind="ExternalInput")
    wu_d = nc.dram_tensor("wu", [H, D], I32, kind="ExternalInput")
    wd_d = nc.dram_tensor("wd", [D, H], I32, kind="ExternalInput")
    out_d = nc.dram_tensor("out", [SEQ_PER_CORE, S, D], F32, kind="ExternalOutput")
    y_dram = nc.dram_tensor("ydram", [S, D], F32, kind="Internal")
    xbf_dram = nc.dram_tensor("xbf", [S, D], BF16, kind="Internal")
    ybf_dram = nc.dram_tensor("ybf", [S, D], BF16, kind="Internal")
    wgbf_d = nc.dram_tensor("wgbf", [H, D], BF16, kind="Internal")
    wubf_d = nc.dram_tensor("wubf", [H, D], BF16, kind="Internal")
    wdbf_d = nc.dram_tensor("wdbf", [D, H], BF16, kind="Internal")

    PF = nc.alloc_psum_tensor("PF", [P, 3072], F32)
    PB = nc.alloc_psum_tensor("PB", [P, 2048], BF16)
    assert nc.lookup_mloc(PF).bank == 0 and nc.lookup_mloc(PB).bank == 6

    def pf(col0, width):
        ap = PF[:, col0:col0 + width]
        keys = tuple(f"F{bk}" for bk in range(col0 // 512, (col0 + width - 1) // 512 + 1))
        return ap, keys

    def pbank(half, col0, width):
        ap = PB[:, half * 1024 + col0: half * 1024 + col0 + width]
        return ap, (f"B{6 + half}",)

    b = Builder(nc)
    ar = Arena(nc, b)

    identb = ar.alloc("identb", [P, P], BF16)
    maskt = ar.alloc("maskt", [P, 4, 512], F32)
    nmax = ar.alloc("nmax", [P, 2], F32)
    rs_all = ar.alloc("rs_all", [P, NT], F32)
    ri_all = ar.alloc("ri_all", [P, NT], F32)
    s1 = ar.alloc("s1", [P, 2], F32)
    s2 = ar.alloc("s2", [P, 2], F32)
    negmean = ar.alloc("negmean", [P, 2], F32)
    nmean2 = ar.alloc("nmean2", [P, 2], F32)
    var2 = ar.alloc("var2", [P, 2], F32)
    lnl = ar.alloc("lnl", [P, 2], F32)
    rstd = ar.alloc("rstd", [P, 2], F32)
    sqbuf = ar.alloc("sqbuf", [P, D], F32)
    epsb = ar.alloc("epsb", [P, 1], F32)
    xs = ar.alloc("xs", [P, 2, D], F32)
    b.op("pool", lambda e: e.memset(epsb[:], EPS), writes=[("epsb",)])
    b.op("pool", lambda e: e.memset(identb[:], 0.0), writes=[("identb",)])
    b.op("pool", lambda e: e.affine_select(out=identb[:], in_=identb[:], compare_op=ALU.not_equal,
                                           fill=1.0, base=0, pattern=[[-1, P]], channel_multiplier=1),
         reads=[("identb",)], writes=[("identb",)])
    for j in range(4):
        b.op("pool", lambda e, j=j: e.memset(maskt[:, j, :], 0.0), writes=[("maskt", j)])
        b.op("pool", lambda e, j=j: e.affine_select(out=maskt[:, j, :], in_=maskt[:, j, :],
                                                    compare_op=ALU.is_ge, fill=-1e30, base=j * P,
                                                    pattern=[[-1, 512]], channel_multiplier=1),
             reads=[("maskt", j)], writes=[("maskt", j)])

    def emit_ln(src_ap, src_keys, dst_ap, dst_keys, sl, stats_eng):
        b.op(stats_eng, lambda e: e.reduce_sum(s1[:, sl:sl + 1], src_ap, axis=AX.X),
             reads=src_keys, writes=[("s1", sl)])
        b.op(stats_eng, lambda e: e.scalar_tensor_tensor(sqbuf[:], src_ap, 1.0, src_ap, ALU.mult, ALU.mult,
                                                         accum_out=s2[:, sl:sl + 1]),
             reads=src_keys, writes=[("sqbuf",), ("s2", sl)])
        b.op("dve", lambda e: e.tensor_scalar_mul(negmean[:, sl:sl + 1], s1[:, sl:sl + 1], -1.0 / D),
             reads=[("s1", sl)], writes=[("negmean", sl)])
        b.op("dve", lambda e: e.tensor_scalar(nmean2[:, sl:sl + 1], negmean[:, sl:sl + 1],
                                              negmean[:, sl:sl + 1], -1.0, ALU.mult, ALU.mult),
             reads=[("negmean", sl)], writes=[("nmean2", sl)])
        b.op("dve", lambda e: e.tensor_scalar(var2[:, sl:sl + 1], s2[:, sl:sl + 1],
                                              1.0 / D, nmean2[:, sl:sl + 1], ALU.mult, ALU.add),
             reads=[("s2", sl), ("nmean2", sl)], writes=[("var2", sl)])
        b.op("act", lambda e: e.activation(lnl[:, sl:sl + 1], var2[:, sl:sl + 1], AF.Ln, bias=epsb[:], scale=1.0),
             reads=[("var2", sl), ("epsb",)], writes=[("lnl", sl)])
        b.op("act", lambda e: e.activation(rstd[:, sl:sl + 1], lnl[:, sl:sl + 1], AF.Exp, scale=-0.5),
             reads=[("lnl", sl)], writes=[("rstd", sl)])
        b.op("dve", lambda e: e.tensor_scalar(dst_ap, src_ap, negmean[:, sl:sl + 1], rstd[:, sl:sl + 1],
                                              ALU.add, ALU.mult),
             reads=list(src_keys) + [("negmean", sl), ("rstd", sl)], writes=dst_keys)

    def emit_seq(s):
        sfx = f"_s{s}"

        xT = ar.alloc("xT" + sfx, [P, ND, S], BF16)
        wqb = ar.alloc("wqb" + sfx, [P, ND, D], BF16)
        wkb = ar.alloc("wkb" + sfx, [P, ND, D], BF16)

        b.dma("pool", lambda e: e.dma_start(out=xbf_dram.ap(), in_=x_d.ap()[s]),
              writes=[("xbf",)], key="xbfc")
        b.dma("pool", lambda e: e.dma_start(out=wqb[:], in_=wq_d.ap().rearrange("(ec p) k -> p ec k", p=P)),
              writes=[("wqb" + sfx,)], key="wq")
        b.dma("pool", lambda e: e.dma_start(out=wkb[:], in_=wk_d.ap().rearrange("(ec p) k -> p ec k", p=P)),
              writes=[("wkb" + sfx,)], key="wk")
        for dc in range(ND):
            b.dma("sp", lambda e, dc=dc: e.dma_start_transpose(xT[:, dc, :], xbf_dram.ap()[:, dc * P:(dc + 1) * P]),
                  reads=[("xbf",)], writes=[("xT" + sfx, dc)], key="xTt", coarse=True)

        def project_fmaj(wb, wname, dstname, dst):
            for m in range(ND):
                for ec in range(ND):
                    for n in range(4):
                        ap, kk = pf(n * 512, 512)
                        b.op("pe", lambda e, ap=ap, m=m, ec=ec, n=n, wb=wb:
                             e.matmul(ap, wb[:, ec, m * P:(m + 1) * P], xT[:, ec, n * 512:(n + 1) * 512],
                                      start=(ec == 0), stop=(ec == ND - 1)),
                             reads=[(wname,), ("xT" + sfx, ec)], writes=list(kk))
                for n in range(4):
                    ap, kk = pf(n * 512, 512)
                    b.op("act", lambda e, dst=dst, m=m, n=n, ap=ap: e.copy(dst[:, m, n * 512:(n + 1) * 512], ap),
                         reads=list(kk), writes=[(dstname, m, n)])

        qT = ar.alloc("qT" + sfx, [P, ND, S], BF16)
        kT = ar.alloc("kT" + sfx, [P, ND, S], BF16)
        project_fmaj(wqb, "wqb" + sfx, "qT" + sfx, qT)
        project_fmaj(wkb, "wkb" + sfx, "kT" + sfx, kT)
        ar.free("wqb" + sfx)
        ar.free("wkb" + sfx)

        wvb = ar.alloc("wvb" + sfx, [P, ND, D], BF16)
        b.dma("pool", lambda e: e.dma_start(out=wvb[:], in_=wv_d.ap().rearrange("(ec p) k -> p ec k", p=P)),
              writes=[("wvb" + sfx,)], key="wv")
        vtm = ar.alloc("v" + sfx, [P, NT, D], BF16)
        for t in range(NT):
            for ec in range(ND):
                for half in range(2):
                    ap, kk = pf(((t % 2) * 2 + half) * 512, 512)
                    b.op("pe", lambda e, ap=ap, ec=ec, t=t, half=half:
                         e.matmul(ap, xT[:, ec, t * P:(t + 1) * P], wvb[:, ec, half * 512:(half + 1) * 512],
                                  start=(ec == 0), stop=(ec == ND - 1)),
                         reads=[("wvb" + sfx,), ("xT" + sfx, ec)], writes=list(kk))
            for half in range(2):
                ap, kk = pf(((t % 2) * 2 + half) * 512, 512)
                b.op("act", lambda e, t=t, half=half, ap=ap: e.copy(vtm[:, t, half * 512:(half + 1) * 512], ap),
                     reads=list(kk), writes=[("v" + sfx, t)])
        ar.free("wvb" + sfx)
        ar.free("xT" + sfx)

        aoT = ar.alloc("aoT" + sfx, [P, ND, S], BF16)
        wob = ar.alloc("wob" + sfx, [P, ND, D], BF16)
        b.dma("pool", lambda e: e.dma_start(out=wob[:], in_=wo_d.ap().rearrange("(ec p) k -> p ec k", p=P)),
              writes=[("wob" + sfx,)], key="wo")
        if s == 0:
            b.dma("pool", lambda e: e.dma_start(out=wgbf_d.ap(), in_=wg_d.ap()), writes=[("wgbf",)], key="wgc")
            b.dma("pool", lambda e: e.dma_start(out=wubf_d.ap(), in_=wu_d.ap()), writes=[("wubf",)], key="wuc")
            b.dma("pool", lambda e: e.dma_start(out=wdbf_d.ap(), in_=wd_d.ap()), writes=[("wdbf",)], key="wdc")
        scb = ar.alloc("sc" + sfx, [P, S], F32)
        Eb = ar.alloc("E" + sfx, [P, 4, S], BF16)
        ET = ar.alloc("ET" + sfx, [P, NT, 512], BF16)

        def emit_scores_softmax(t):
            g = t // 4
            wlen = (g + 1) * 512
            j = t % 4
            sl = t % 2
            for nb in range(g + 1):
                ap, kk = pf(nb * 512, 512)
                for ec in range(ND):
                    b.op("pe", lambda e, ap=ap, ec=ec, t=t, nb=nb:
                         e.matmul(ap, qT[:, ec, t * P:(t + 1) * P], kT[:, ec, nb * 512:(nb + 1) * 512],
                                  start=(ec == 0), stop=(ec == ND - 1)),
                         reads=[("qT" + sfx, ec, t // 4), ("kT" + sfx, ec, nb)], writes=list(kk))
                if nb == g:
                    b.op("dve", lambda e, ap=ap, nb=nb, j=j:
                         e.scalar_tensor_tensor(scb[:, nb * 512:(nb + 1) * 512], ap, INV_SQRT_DK,
                                                maskt[:, j, :], ALU.mult, ALU.add),
                         reads=list(kk) + [("maskt", j)], writes=[("sc" + sfx, nb)])
                else:
                    b.op("act", lambda e, ap=ap, nb=nb:
                         e.activation(scb[:, nb * 512:(nb + 1) * 512], ap, AF.Copy, scale=INV_SQRT_DK),
                         reads=list(kk), writes=[("sc" + sfx, nb)])
            sckeys = [("sc" + sfx, nb) for nb in range(g + 1)]
            b.op("dve", lambda e, wlen=wlen, sl=sl:
                 e.reduce_max(nmax[:, sl:sl + 1], scb[:, 0:wlen], axis=AX.X, negate=True),
                 reads=sckeys, writes=[("nmax", sl)])
            b.op("act", lambda e, wlen=wlen, sl=sl, t=t, j=j:
                 e.activation(Eb[:, j, 0:wlen], scb[:, 0:wlen], AF.Exp, bias=nmax[:, sl:sl + 1], scale=1.0,
                              accum_out=rs_all[:, t:t + 1]),
                 reads=sckeys + [("nmax", sl)], writes=[("E" + sfx, j), ("rs", t)])

        def emit_T(t):
            g = t // 4
            ncs = 4 * (g + 1)
            j = t % 4
            for c0 in range(0, ncs, 8):
                cb = min(8, ncs - c0)
                half = (c0 // 8) % 2
                for ci in range(cb):
                    cs = c0 + ci
                    apb, kkb = pbank(half, ci * P, P)
                    b.op("pe", lambda e, apb=apb, cs=cs, j=j:
                         e.transpose(apb, Eb[:, j, cs * P:(cs + 1) * P], identb[:]),
                         reads=[("E" + sfx, j), ("identb",)], writes=list(kkb))
                apb, kkb = pbank(half, 0, cb * P)
                dst = ET[:, c0:c0 + cb, j * P:(j + 1) * P]
                b.op("dve", lambda e, dst=dst, apb=apb: e.tensor_copy(dst, apb),
                     reads=list(kkb), writes=[("ET" + sfx, c0 + ci, j) for ci in range(cb)])

        def emit_AV(g):
            ncs = 4 * (g + 1)
            for vc in range(ND):
                ap, kk = pf((4 + vc % 2) * 512, 512)
                for cs in range(ncs):
                    b.op("pe", lambda e, ap=ap, cs=cs, vc=vc, ncs=ncs:
                         e.matmul(ap, vtm[:, cs, vc * P:(vc + 1) * P], ET[:, cs, :],
                                  start=(cs == 0), stop=(cs == ncs - 1)),
                         reads=[("v" + sfx, cs)] + [("ET" + sfx, cs, j) for j in range(4)],
                         writes=list(kk))
                b.op("dve", lambda e, ap=ap, vc=vc, g=g: e.tensor_copy(aoT[:, vc, g * 512:(g + 1) * 512], ap),
                     reads=list(kk), writes=[("aoT" + sfx, vc, g)])

        for g in range(NG):
            for j in range(4):
                emit_scores_softmax(4 * g + j)
            for j in range(4):
                emit_T(4 * g + j)
            emit_AV(g)
        b.op("dve", lambda e: e.reciprocal(ri_all[:], rs_all[:]),
             reads=[("rs", t) for t in range(NT)], writes=[("ri",)])

        ar.free("qT" + sfx)
        ar.free("kT" + sfx)
        ar.free("v" + sfx)
        ar.free("sc" + sfx)
        ar.free("E" + sfx)
        ar.free("ET" + sfx)

        r1 = ar.alloc("r1" + sfx, [P, 2, D], F32)
        ybuf = ar.alloc("y" + sfx, [P, 2, D], F32)
        yT = ar.alloc("yT" + sfx, [P, ND, S], BF16)

        for t in range(NT):
            sl = t % 2
            bz = 0 if t % 2 == 0 else 2
            b.dma("sp", lambda e, t=t, sl=sl: e.dma_start(out=xs[:, sl, :], in_=x_d.ap()[s, t * P:(t + 1) * P, :]),
                  writes=[("xs", sl)], key=f"xs{sl}")
            for half in range(2):
                for vc in range(ND):
                    ap, kk = pf((bz + half) * 512, 512)
                    b.op("pe", lambda e, ap=ap, vc=vc, t=t, half=half:
                         e.matmul(ap, aoT[:, vc, t * P:(t + 1) * P], wob[:, vc, half * 512:(half + 1) * 512],
                                  start=(vc == 0), stop=(vc == ND - 1)),
                         reads=[("wob" + sfx,), ("aoT" + sfx, vc, t // 4)], writes=list(kk))
            apz, kkz = pf(bz * 512, 1024)
            b.op("dve", lambda e, apz=apz, sl=sl, t=t:
                 e.scalar_tensor_tensor(r1[:, sl, :], apz, ri_all[:, t:t + 1], xs[:, sl, :], ALU.mult, ALU.add),
                 reads=list(kkz) + [("xs", sl), ("ri",)], writes=[("r1" + sfx, sl)])
            emit_ln(r1[:, sl, :], [("r1" + sfx, sl)], ybuf[:, sl, :], [("y" + sfx, sl)], sl, "dve")
            b.dma("sp", lambda e, t=t, sl=sl: e.dma_start(out=y_dram.ap()[t * P:(t + 1) * P, :], in_=ybuf[:, sl, :]),
                  reads=[("y" + sfx, sl)], writes=[("ydram", t)], key=f"yst{sl}")
            b.dma("pool", lambda e, t=t, sl=sl: e.dma_start(out=ybf_dram.ap()[t * P:(t + 1) * P, :], in_=ybuf[:, sl, :]),
                  reads=[("y" + sfx, sl)], writes=[("ybf", t)], key=f"ybst{sl}")
            if t % 8 == 7:
                half_seq = t // 8
                for dc in range(ND):
                    b.dma("sp", lambda e, dc=dc, hs=half_seq:
                          e.dma_start_transpose(yT[:, dc, hs * 1024:(hs + 1) * 1024],
                                                ybf_dram.ap()[hs * 1024:(hs + 1) * 1024, dc * P:(dc + 1) * P]),
                          reads=[("ybf", tt) for tt in range(half_seq * 8, half_seq * 8 + 8)],
                          writes=[("yT" + sfx, dc, half_seq)], key=f"yTt{half_seq}", coarse=True)

        ar.free("aoT" + sfx)
        ar.free("wob" + sfx)
        ar.free("r1" + sfx)
        ar.free("y" + sfx)

        h_actT = ar.alloc("h" + sfx, [P, NH, S], BF16)
        wgTb = ar.alloc("wgTb" + sfx, [P, 2, ND, 512], BF16)
        wuTb = ar.alloc("wuTb" + sfx, [P, 2, ND, 512], BF16)
        sg = ar.alloc("sg" + sfx, [P, 2, 512], BF16)

        for hgrp in range(NH // 4):
            slw = hgrp % 2
            for dc in range(ND):
                b.dma("sp", lambda e, dc=dc, hgrp=hgrp, slw=slw:
                      e.dma_start_transpose(wgTb[:, slw, dc, :],
                                            wgbf_d.ap()[hgrp * 512:(hgrp + 1) * 512, dc * P:(dc + 1) * P]),
                      reads=[("wgbf",)], writes=[("wgTb" + sfx, slw)], key=f"wgt{slw}", coarse=True)
                b.dma("sp", lambda e, dc=dc, hgrp=hgrp, slw=slw:
                      e.dma_start_transpose(wuTb[:, slw, dc, :],
                                            wubf_d.ap()[hgrp * 512:(hgrp + 1) * 512, dc * P:(dc + 1) * P]),
                      reads=[("wubf",)], writes=[("wuTb" + sfx, slw)], key=f"wut{slw}", coarse=True)
            for hl in range(4):
                hc = hgrp * 4 + hl
                for npair in range(2):
                    for dc in range(ND):
                        for i in (0, 1):
                            n = 2 * npair + i
                            apg, kkg = pf((i) * 512, 512)
                            b.op("pe", lambda e, apg=apg, dc=dc, n=n, slw=slw, hl=hl:
                                 e.matmul(apg, wgTb[:, slw, dc, hl * P:(hl + 1) * P],
                                          yT[:, dc, n * 512:(n + 1) * 512],
                                          start=(dc == 0), stop=(dc == ND - 1)),
                                 reads=[("wgTb" + sfx, slw), ("yT" + sfx, dc, n // 2)],
                                 writes=list(kkg))
                    for dc in range(ND):
                        for i in (0, 1):
                            n = 2 * npair + i
                            apu, kku = pf((2 + i) * 512, 512)
                            b.op("pe", lambda e, apu=apu, dc=dc, n=n, slw=slw, hl=hl:
                                 e.matmul(apu, wuTb[:, slw, dc, hl * P:(hl + 1) * P],
                                          yT[:, dc, n * 512:(n + 1) * 512],
                                          start=(dc == 0), stop=(dc == ND - 1)),
                                 reads=[("wuTb" + sfx, slw), ("yT" + sfx, dc, n // 2)],
                                 writes=list(kku))
                    for i in (0, 1):
                        n = 2 * npair + i
                        apg, kkg = pf(i * 512, 512)
                        apu, kku = pf((2 + i) * 512, 512)
                        b.op("act", lambda e, apg=apg, n=n: e.activation(sg[:, n % 2, :], apg, AF.Silu, scale=DEQ),
                             reads=list(kkg), writes=[("sg" + sfx, n % 2)])
                        b.op("dve", lambda e, apu=apu, n=n, hc=hc:
                             e.tensor_tensor(h_actT[:, hc, n * 512:(n + 1) * 512], apu, sg[:, n % 2, :], ALU.mult),
                             reads=list(kku) + [("sg" + sfx, n % 2)], writes=[("h" + sfx, hc, n)])

        ar.free("yT" + sfx)
        ar.free("wgTb" + sfx)
        ar.free("wuTb" + sfx)
        ar.free("sg" + sfx)

        wdT = ar.alloc("wdT" + sfx, [P, NH, D], BF16)
        ys = ar.alloc("ys" + sfx, [P, 2, D], F32)
        r2 = ar.alloc("r2" + sfx, [P, 2, D], F32)
        outb = ar.alloc("outb" + sfx, [P, 2, D], F32)

        for hc in range(NH):
            b.dma("sp", lambda e, hc=hc: e.dma_start_transpose(wdT[:, hc, :], wdbf_d.ap()[:, hc * P:(hc + 1) * P]),
                  reads=[("wdbf",)], writes=[("wdT" + sfx, hc)], key="wdt", coarse=True)

        for t in range(NT):
            sl = t % 2
            bz = 0 if t % 2 == 0 else 2
            b.dma("sp", lambda e, t=t, sl=sl: e.dma_start(out=ys[:, sl, :], in_=y_dram.ap()[t * P:(t + 1) * P, :]),
                  reads=[("ydram", t)], writes=[("ys" + sfx, sl)], key=f"ys{sl}")
            for hc in range(NH):
                for half in range(2):
                    ap, kk = pf((bz + half) * 512, 512)
                    b.op("pe", lambda e, ap=ap, hc=hc, t=t, half=half:
                         e.matmul(ap, h_actT[:, hc, t * P:(t + 1) * P], wdT[:, hc, half * 512:(half + 1) * 512],
                                  start=(hc == 0), stop=(hc == NH - 1)),
                         reads=[("h" + sfx, hc, t // 4), ("wdT" + sfx, hc)], writes=list(kk))
            apz, kkz = pf(bz * 512, 1024)
            b.op("dve", lambda e, apz=apz, sl=sl:
                 e.scalar_tensor_tensor(r2[:, sl, :], apz, DEQ * DEQ, ys[:, sl, :], ALU.mult, ALU.add),
                 reads=list(kkz) + [("ys" + sfx, sl)], writes=[("r2" + sfx, sl)])
            emit_ln(r2[:, sl, :], [("r2" + sfx, sl)], outb[:, sl, :], [("outb" + sfx, sl)], sl, "dve")
            b.dma("sp", lambda e, t=t, sl=sl: e.dma_start(out=out_d.ap()[s, t * P:(t + 1) * P, :], in_=outb[:, sl, :]),
                  reads=[("outb" + sfx, sl)], key=f"ost{sl}")

        ar.free("wdT" + sfx)
        ar.free("h" + sfx)
        ar.free("ys" + sfx)
        ar.free("r2" + sfx)
        ar.free("outb" + sfx)

    for s in range(SEQ_PER_CORE):
        emit_seq(s)

    stats = b.finalize_and_emit(final_wait_keys=["ost0", "ost1"])
    stats["sbuf_peak"] = ar.peak
    return nc, stats


def kernel(L0_attn_x, L0_attn_Wq, L0_attn_Wk, L0_attn_Wv, L0_attn_Wo,
           L0_ffn_w_gate, L0_ffn_w_up, L0_ffn_w_down):
    x = np.ascontiguousarray(np.asarray(L0_attn_x, dtype=np.float32))
    wq = np.ascontiguousarray(np.asarray(L0_attn_Wq, dtype=np.float32))
    wk = np.ascontiguousarray(np.asarray(L0_attn_Wk, dtype=np.float32))
    wv = np.ascontiguousarray(np.asarray(L0_attn_Wv, dtype=np.float32))
    wo = np.ascontiguousarray(np.asarray(L0_attn_Wo, dtype=np.float32))
    wg = np.ascontiguousarray(np.asarray(L0_ffn_w_gate, dtype=np.int32))
    wu = np.ascontiguousarray(np.asarray(L0_ffn_w_up, dtype=np.int32))
    wd = np.ascontiguousarray(np.asarray(L0_ffn_w_down, dtype=np.int32))

    nc, stats = build_program()
    in_maps = []
    for c in range(N_CORES):
        in_maps.append({
            "x": x[c * SEQ_PER_CORE:(c + 1) * SEQ_PER_CORE],
            "wq": wq, "wk": wk, "wv": wv, "wo": wo,
            "wg": wg, "wu": wu, "wd": wd,
        })
    res = run_bass_kernel_spmd(nc, in_maps, core_ids=list(range(N_CORES)))
    out = np.concatenate([r["out"] for r in res.results], axis=0)
    return out.astype(np.float32)


if __name__ == "__main__":
    nc, stats = build_program()
    print("build ok:", stats)


# version 14
# speedup vs baseline: 1.1385x; 1.0203x over previous
import bisect
import numpy as np
import ml_dtypes
import concourse.bass as bass
import concourse.mybir as mybir
from concourse.bass_utils import run_bass_kernel_spmd

F32 = mybir.dt.float32
BF16 = mybir.dt.bfloat16
I32 = mybir.dt.int32
U8 = mybir.dt.uint8
AF = mybir.ActivationFunctionType
ALU = mybir.AluOpType
AX = mybir.AxisListType
NPBF = ml_dtypes.bfloat16

P = 128
S = 2048
D = 1024
H = 3072
NT = S // P
ND = D // P
NH = H // P
NG = NT // 4
SEQ_PER_CORE = 2
N_CORES = 8
INV_SQRT_DK = 1.0 / 32.0
DEQ = 1.0 / 127.0
EPS = 1e-5

BLOCK_METHOD = {"pe": "tensor", "act": "scalar", "dve": "vector", "pool": "gpsimd", "sp": "sync"}


class _Op:
    __slots__ = ("i", "eng", "fn", "deps", "name", "flag", "dma_key", "sem_id", "val", "know")

    def __init__(self, i, eng, fn, deps, name, dma_key):
        self.i, self.eng, self.fn, self.deps, self.name, self.dma_key = i, eng, fn, deps, name, dma_key
        self.flag = False
        self.sem_id = None
        self.val = None
        self.know = None


class Builder:

    def __init__(self, nc):
        self.nc = nc
        self.ops = []
        self.lw = {}
        self.rd = {}
        self.name_keys = {}
        self.alias_deps = {}
        self.coarse_keys = set()

    @staticmethod
    def _key_name(k):
        return k[0] if isinstance(k, tuple) else k

    def op(self, eng, fn, reads=(), writes=(), name="", dma_key=None):
        i = len(self.ops)
        deps = set()
        for r in reads:
            if r in self.lw:
                deps.add(self.lw[r])
        for w in writes:
            if w in self.lw:
                deps.add(self.lw[w])
            for r_ in self.rd.get(w, ()):
                deps.add(r_)
            ad = self.alias_deps.get(self._key_name(w))
            if ad:
                deps |= ad
        deps.discard(i)
        o = _Op(i, eng, fn, deps, name, dma_key)
        self.ops.append(o)
        for r in reads:
            self.rd.setdefault(r, []).append(i)
            self.name_keys.setdefault(self._key_name(r), set()).add(r)
        for w in writes:
            self.lw[w] = i
            self.rd[w] = []
            self.name_keys.setdefault(self._key_name(w), set()).add(w)
        return i

    def dma(self, queue, fn, reads=(), writes=(), key=None, name="", coarse=False):
        if key is None:
            key = "dq_" + queue
        if coarse:
            self.coarse_keys.add(key)
        return self.op(queue, fn, reads, writes, name, dma_key=key)

    def pending_ops_of(self, tensor_name):
        out = set()
        for k in self.name_keys.get(tensor_name, ()):
            if k in self.lw:
                out.add(self.lw[k])
            out.update(self.rd.get(k, ()))
        return out

    def add_alias_deps(self, tensor_name, dep_ops):
        if dep_ops:
            self.alias_deps.setdefault(tensor_name, set()).update(dep_ops)

    def finalize_and_emit(self, final_wait_keys=()):
        nc = self.nc
        ops = self.ops
        for o in ops:
            for d in o.deps:
                p = ops[d]
                if p.dma_key is None and not (p.eng == o.eng and p.eng in ("pe", "sp")):
                    p.flag = True
        eng_sems = {}
        dma_sems = {}
        for o in ops:
            if o.dma_key is not None:
                if o.dma_key not in dma_sems:
                    dma_sems[o.dma_key] = nc.alloc_semaphore(name=("sd_" + str(o.dma_key))[:30])
            elif o.flag and o.eng not in eng_sems:
                eng_sems[o.eng] = nc.alloc_semaphore(name="se_" + o.eng)
        eng_counts = {}
        dma_counts = {}
        for o in ops:
            if o.dma_key is not None:
                c = dma_counts.get(o.dma_key, 0) + 16
                dma_counts[o.dma_key] = c
                o.sem_id = ("d", o.dma_key)
                o.val = c
            elif o.flag:
                c = eng_counts.get(o.eng, 0) + 1
                eng_counts[o.eng] = c
                o.sem_id = ("e", o.eng)
                o.val = c

        key_idx = {}
        for o in ops:
            if o.dma_key in self.coarse_keys:
                key_idx.setdefault(o.dma_key, []).append(o.i)

        def eff_val(p, oi):
            if p.dma_key in self.coarse_keys:
                return 16 * bisect.bisect_left(key_idx[p.dma_key], oi)
            return p.val

        last_on_eng = {}
        for o in ops:
            know = dict(last_on_eng.get(o.eng, {}))
            for d in sorted(o.deps):
                p = ops[d]
                if p.sem_id is None:
                    continue
                v = eff_val(p, o.i)
                if know.get(p.sem_id, 0) < v:
                    know[p.sem_id] = v
                for k, v2 in p.know.items():
                    if know.get(k, 0) < v2:
                        know[k] = v2
            o.know = know
            last_on_eng[o.eng] = know

        ops_by_eng = {}
        for o in ops:
            ops_by_eng.setdefault(o.eng, []).append(o)
        n_waits = [0]

        def emit_for(eng_key):
            def body(eng):
                know = {}
                for o in ops_by_eng[eng_key]:
                    for d in sorted(o.deps):
                        p = ops[d]
                        if p.sem_id is None:
                            continue
                        v = eff_val(p, o.i)
                        if know.get(p.sem_id, 0) >= v:
                            continue
                        sem = dma_sems[p.sem_id[1]] if p.sem_id[0] == "d" else eng_sems[p.sem_id[1]]
                        eng.wait_ge(sem, v)
                        n_waits[0] += 1
                        know[p.sem_id] = v
                        for k, v2 in p.know.items():
                            if know.get(k, 0) < v2:
                                know[k] = v2
                    ins = o.fn(eng)
                    if o.sem_id is not None:
                        ins.then_inc(dma_sems[o.sem_id[1]] if o.sem_id[0] == "d" else eng_sems[o.sem_id[1]],
                                     16 if o.sem_id[0] == "d" else 1)
                    for k, v2 in o.know.items():
                        if know.get(k, 0) < v2:
                            know[k] = v2
                if eng_key == "sp":
                    for k in final_wait_keys:
                        if k in dma_counts and know.get(("d", k), 0) < dma_counts[k]:
                            eng.wait_ge(dma_sems[k], dma_counts[k])
            return body

        with nc.Block() as block:
            keys = set(ops_by_eng) | ({"sp"} if final_wait_keys else set())
            for eng_key in keys:
                ops_by_eng.setdefault(eng_key, [])
                getattr(block, BLOCK_METHOD[eng_key])(emit_for(eng_key))
        return {"n_ops": len(ops), "n_waits": n_waits[0],
                "n_by_eng": {k: len(v) for k, v in ops_by_eng.items()}}


class Arena:

    def __init__(self, nc, b):
        self.nc = nc
        self.b = b
        avail = nc.sbuf_bytes_remaining - 2048
        self.backing = nc.alloc_sbuf_tensor("arena", [P, avail], U8)
        self.base = nc.lookup_mloc(self.backing).addr
        self.size = avail
        self.free_list = [[0, avail, set()]]
        self.live = {}
        self.peak = 0

    def alloc(self, name, shape, dtype):
        nbytes = int(np.prod(shape[1:])) * mybir.dt.size(dtype)
        nbytes = (nbytes + 31) // 32 * 32
        for ent in sorted(self.free_list, key=lambda e: e[1]):
            if ent[1] >= nbytes:
                off = ent[0]
                deps = set(ent[2])
                ent[0] += nbytes
                ent[1] -= nbytes
                if ent[1] == 0:
                    self.free_list.remove(ent)
                for nm, (o2, n2) in self.live.items():
                    assert off + nbytes <= o2 or o2 + n2 <= off, \
                        f"arena overlap: {name}@({off},{nbytes}) vs {nm}@({o2},{n2})"
                h = self.nc.alloc_sbuf_tensor_at(name, list(shape), dtype, offset=self.base + off)
                self.live[name] = (off, nbytes)
                self.peak = max(self.peak, max(o + n for o, n in self.live.values()))
                self.b.add_alias_deps(name, deps)
                return h
        raise MemoryError(f"arena: no space for {name} ({nbytes}B/part); "
                          f"free={[(e[0], e[1]) for e in self.free_list]}")

    def free(self, name):
        off, nbytes = self.live.pop(name)
        deps = self.b.pending_ops_of(name)
        self.free_list.append([off, nbytes, deps])
        self.free_list.sort(key=lambda e: e[0])
        merged = [self.free_list[0]]
        for ent in self.free_list[1:]:
            last = merged[-1]
            if last[0] + last[1] == ent[0]:
                last[1] += ent[1]
                last[2] = last[2] | ent[2]
            else:
                merged.append(ent)
        self.free_list = merged


def build_program():
    nc = bass.Bass("TRN2", target_bir_lowering=False)

    x_d = nc.dram_tensor("x", [SEQ_PER_CORE, S, D], F32, kind="ExternalInput")
    wq_d = nc.dram_tensor("wq", [D, D], F32, kind="ExternalInput")
    wk_d = nc.dram_tensor("wk", [D, D], F32, kind="ExternalInput")
    wv_d = nc.dram_tensor("wv", [D, D], F32, kind="ExternalInput")
    wo_d = nc.dram_tensor("wo", [D, D], F32, kind="ExternalInput")
    wg_d = nc.dram_tensor("wg", [H, D], I32, kind="ExternalInput")
    wu_d = nc.dram_tensor("wu", [H, D], I32, kind="ExternalInput")
    wd_d = nc.dram_tensor("wd", [D, H], I32, kind="ExternalInput")
    out_d = nc.dram_tensor("out", [SEQ_PER_CORE, S, D], F32, kind="ExternalOutput")
    y_dram = nc.dram_tensor("ydram", [S, D], F32, kind="Internal")
    xbf_dram = nc.dram_tensor("xbf", [S, D], BF16, kind="Internal")
    ybf_dram = nc.dram_tensor("ybf", [S, D], BF16, kind="Internal")
    wgbf_d = nc.dram_tensor("wgbf", [H, D], BF16, kind="Internal")
    wubf_d = nc.dram_tensor("wubf", [H, D], BF16, kind="Internal")
    wdbf_d = nc.dram_tensor("wdbf", [D, H], BF16, kind="Internal")

    PF = nc.alloc_psum_tensor("PF", [P, 3072], F32)
    PB = nc.alloc_psum_tensor("PB", [P, 2048], BF16)
    assert nc.lookup_mloc(PF).bank == 0 and nc.lookup_mloc(PB).bank == 6

    def pf(col0, width):
        ap = PF[:, col0:col0 + width]
        keys = tuple(f"F{bk}" for bk in range(col0 // 512, (col0 + width - 1) // 512 + 1))
        return ap, keys

    def pbank(half, col0, width):
        ap = PB[:, half * 1024 + col0: half * 1024 + col0 + width]
        return ap, (f"B{6 + half}",)

    b = Builder(nc)
    ar = Arena(nc, b)

    identb = ar.alloc("identb", [P, P], BF16)
    maskt = ar.alloc("maskt", [P, 4, 512], F32)
    nmax = ar.alloc("nmax", [P, 2], F32)
    rs_all = ar.alloc("rs_all", [P, NT], F32)
    ri_all = ar.alloc("ri_all", [P, NT], F32)
    bnst = ar.alloc("bnst", [P, 2, 12], F32)
    mv = ar.alloc("mv", [P, 2, 2], F32)
    negmean = ar.alloc("negmean", [P, 2], F32)
    lnl = ar.alloc("lnl", [P, 2], F32)
    rstd = ar.alloc("rstd", [P, 2], F32)
    epsb = ar.alloc("epsb", [P, 1], F32)
    xs = ar.alloc("xs", [P, 2, D], F32)
    b.op("pool", lambda e: e.memset(epsb[:], EPS), writes=[("epsb",)])
    b.op("pool", lambda e: e.memset(identb[:], 0.0), writes=[("identb",)])
    b.op("pool", lambda e: e.affine_select(out=identb[:], in_=identb[:], compare_op=ALU.not_equal,
                                           fill=1.0, base=0, pattern=[[-1, P]], channel_multiplier=1),
         reads=[("identb",)], writes=[("identb",)])
    for j in range(4):
        b.op("pool", lambda e, j=j: e.memset(maskt[:, j, :], 0.0), writes=[("maskt", j)])
        b.op("pool", lambda e, j=j: e.affine_select(out=maskt[:, j, :], in_=maskt[:, j, :],
                                                    compare_op=ALU.is_ge, fill=-1e30, base=j * P,
                                                    pattern=[[-1, 512]], channel_multiplier=1),
             reads=[("maskt", j)], writes=[("maskt", j)])

    def emit_ln(src_ap, src_keys, dst_ap, dst_keys, sl, stats_eng="dve"):
        b.op("dve", lambda e: e.bn_stats(bnst[:, sl, 0:6], src_ap[:, 0:512]),
             reads=list(src_keys), writes=[("bnst", sl)])
        b.op("dve", lambda e: e.bn_stats(bnst[:, sl, 6:12], src_ap[:, 512:1024]),
             reads=list(src_keys) + [("bnst", sl)], writes=[("bnst", sl)])
        b.op("dve", lambda e: e.bn_aggr(mv[:, sl, :], bnst[:, sl, :]),
             reads=[("bnst", sl)], writes=[("mv", sl)])
        b.op("dve", lambda e: e.tensor_scalar_mul(negmean[:, sl:sl + 1], mv[:, sl, 0:1], -1.0),
             reads=[("mv", sl)], writes=[("negmean", sl)])
        b.op("act", lambda e: e.activation(lnl[:, sl:sl + 1], mv[:, sl, 1:2], AF.Ln, bias=epsb[:], scale=1.0),
             reads=[("mv", sl), ("epsb",)], writes=[("lnl", sl)])
        b.op("act", lambda e: e.activation(rstd[:, sl:sl + 1], lnl[:, sl:sl + 1], AF.Exp, scale=-0.5),
             reads=[("lnl", sl)], writes=[("rstd", sl)])
        b.op("dve", lambda e: e.tensor_scalar(dst_ap, src_ap, negmean[:, sl:sl + 1], rstd[:, sl:sl + 1],
                                              ALU.add, ALU.mult),
             reads=list(src_keys) + [("negmean", sl), ("rstd", sl)], writes=list(dst_keys))

    def emit_seq(s):
        sfx = f"_s{s}"

        xT = ar.alloc("xT" + sfx, [P, ND, S], BF16)
        wqb = ar.alloc("wqb" + sfx, [P, ND, D], BF16)
        wkb = ar.alloc("wkb" + sfx, [P, ND, D], BF16)

        b.dma("pool", lambda e: e.dma_start(out=xbf_dram.ap(), in_=x_d.ap()[s]),
              writes=[("xbf",)], key="xbfc")
        b.dma("pool", lambda e: e.dma_start(out=wqb[:], in_=wq_d.ap().rearrange("(ec p) k -> p ec k", p=P)),
              writes=[("wqb" + sfx,)], key="wq")
        b.dma("pool", lambda e: e.dma_start(out=wkb[:], in_=wk_d.ap().rearrange("(ec p) k -> p ec k", p=P)),
              writes=[("wkb" + sfx,)], key="wk")
        for dc in range(ND):
            b.dma("sp", lambda e, dc=dc: e.dma_start_transpose(xT[:, dc, :], xbf_dram.ap()[:, dc * P:(dc + 1) * P]),
                  reads=[("xbf",)], writes=[("xT" + sfx, dc)], key="xTt", coarse=True)

        def project_fmaj(wb, wname, dstname, dst):
            for m in range(ND):
                for ec in range(ND):
                    for n in range(4):
                        ap, kk = pf(n * 512, 512)
                        b.op("pe", lambda e, ap=ap, m=m, ec=ec, n=n, wb=wb:
                             e.matmul(ap, wb[:, ec, m * P:(m + 1) * P], xT[:, ec, n * 512:(n + 1) * 512],
                                      start=(ec == 0), stop=(ec == ND - 1)),
                             reads=[(wname,), ("xT" + sfx, ec)], writes=list(kk))
                for n in range(4):
                    ap, kk = pf(n * 512, 512)
                    b.op("act", lambda e, dst=dst, m=m, n=n, ap=ap: e.copy(dst[:, m, n * 512:(n + 1) * 512], ap),
                         reads=list(kk), writes=[(dstname, m, n)])

        qT = ar.alloc("qT" + sfx, [P, ND, S], BF16)
        kT = ar.alloc("kT" + sfx, [P, ND, S], BF16)
        project_fmaj(wqb, "wqb" + sfx, "qT" + sfx, qT)
        project_fmaj(wkb, "wkb" + sfx, "kT" + sfx, kT)
        ar.free("wqb" + sfx)
        ar.free("wkb" + sfx)

        wvb = ar.alloc("wvb" + sfx, [P, ND, D], BF16)
        b.dma("pool", lambda e: e.dma_start(out=wvb[:], in_=wv_d.ap().rearrange("(ec p) k -> p ec k", p=P)),
              writes=[("wvb" + sfx,)], key="wv")
        vtm = ar.alloc("v" + sfx, [P, NT, D], BF16)
        for t in range(NT):
            for ec in range(ND):
                for half in range(2):
                    ap, kk = pf(((t % 2) * 2 + half) * 512, 512)
                    b.op("pe", lambda e, ap=ap, ec=ec, t=t, half=half:
                         e.matmul(ap, xT[:, ec, t * P:(t + 1) * P], wvb[:, ec, half * 512:(half + 1) * 512],
                                  start=(ec == 0), stop=(ec == ND - 1)),
                         reads=[("wvb" + sfx,), ("xT" + sfx, ec)], writes=list(kk))
            for half in range(2):
                ap, kk = pf(((t % 2) * 2 + half) * 512, 512)
                b.op("act", lambda e, t=t, half=half, ap=ap: e.copy(vtm[:, t, half * 512:(half + 1) * 512], ap),
                     reads=list(kk), writes=[("v" + sfx, t)])
        ar.free("wvb" + sfx)
        ar.free("xT" + sfx)

        aoT = ar.alloc("aoT" + sfx, [P, ND, S], BF16)
        wob = ar.alloc("wob" + sfx, [P, ND, D], BF16)
        b.dma("pool", lambda e: e.dma_start(out=wob[:], in_=wo_d.ap().rearrange("(ec p) k -> p ec k", p=P)),
              writes=[("wob" + sfx,)], key="wo")
        if s == 0:
            b.dma("pool", lambda e: e.dma_start(out=wgbf_d.ap(), in_=wg_d.ap()), writes=[("wgbf",)], key="wgc")
            b.dma("pool", lambda e: e.dma_start(out=wubf_d.ap(), in_=wu_d.ap()), writes=[("wubf",)], key="wuc")
            b.dma("pool", lambda e: e.dma_start(out=wdbf_d.ap(), in_=wd_d.ap()), writes=[("wdbf",)], key="wdc")
        scb = ar.alloc("sc" + sfx, [P, S], F32)
        Eb = ar.alloc("E" + sfx, [P, 4, S], BF16)
        ET = ar.alloc("ET" + sfx, [P, NT, 512], BF16)

        def emit_scores_softmax(t):
            g = t // 4
            wlen = (g + 1) * 512
            j = t % 4
            sl = t % 2
            for nb in range(g + 1):
                ap, kk = pf(nb * 512, 512)
                for ec in range(ND):
                    b.op("pe", lambda e, ap=ap, ec=ec, t=t, nb=nb:
                         e.matmul(ap, qT[:, ec, t * P:(t + 1) * P], kT[:, ec, nb * 512:(nb + 1) * 512],
                                  start=(ec == 0), stop=(ec == ND - 1)),
                         reads=[("qT" + sfx, ec, t // 4), ("kT" + sfx, ec, nb)], writes=list(kk))
                if nb == g:
                    b.op("dve", lambda e, ap=ap, nb=nb, j=j:
                         e.scalar_tensor_tensor(scb[:, nb * 512:(nb + 1) * 512], ap, INV_SQRT_DK,
                                                maskt[:, j, :], ALU.mult, ALU.add),
                         reads=list(kk) + [("maskt", j)], writes=[("sc" + sfx, nb)])
                else:
                    b.op("act", lambda e, ap=ap, nb=nb:
                         e.activation(scb[:, nb * 512:(nb + 1) * 512], ap, AF.Copy, scale=INV_SQRT_DK),
                         reads=list(kk), writes=[("sc" + sfx, nb)])
            sckeys = [("sc" + sfx, nb) for nb in range(g + 1)]
            b.op("dve", lambda e, wlen=wlen, sl=sl:
                 e.reduce_max(nmax[:, sl:sl + 1], scb[:, 0:wlen], axis=AX.X, negate=True),
                 reads=sckeys, writes=[("nmax", sl)])
            b.op("act", lambda e, wlen=wlen, sl=sl, t=t, j=j:
                 e.activation(Eb[:, j, 0:wlen], scb[:, 0:wlen], AF.Exp, bias=nmax[:, sl:sl + 1], scale=1.0,
                              accum_out=rs_all[:, t:t + 1]),
                 reads=sckeys + [("nmax", sl)], writes=[("E" + sfx, j), ("rs", t)])

        def emit_T(t):
            g = t // 4
            ncs = 4 * (g + 1)
            j = t % 4
            for c0 in range(0, ncs, 8):
                cb = min(8, ncs - c0)
                half = (c0 // 8) % 2
                for ci in range(cb):
                    cs = c0 + ci
                    apb, kkb = pbank(half, ci * P, P)
                    b.op("pe", lambda e, apb=apb, cs=cs, j=j:
                         e.transpose(apb, Eb[:, j, cs * P:(cs + 1) * P], identb[:]),
                         reads=[("E" + sfx, j), ("identb",)], writes=list(kkb))
                apb, kkb = pbank(half, 0, cb * P)
                dst = ET[:, c0:c0 + cb, j * P:(j + 1) * P]
                b.op("dve", lambda e, dst=dst, apb=apb: e.tensor_copy(dst, apb),
                     reads=list(kkb), writes=[("ET" + sfx, c0 + ci, j) for ci in range(cb)])

        def emit_AV(g):
            ncs = 4 * (g + 1)
            for vc in range(ND):
                ap, kk = pf((4 + vc % 2) * 512, 512)
                for cs in range(ncs):
                    b.op("pe", lambda e, ap=ap, cs=cs, vc=vc, ncs=ncs:
                         e.matmul(ap, vtm[:, cs, vc * P:(vc + 1) * P], ET[:, cs, :],
                                  start=(cs == 0), stop=(cs == ncs - 1)),
                         reads=[("v" + sfx, cs)] + [("ET" + sfx, cs, j) for j in range(4)],
                         writes=list(kk))
                b.op("dve", lambda e, ap=ap, vc=vc, g=g: e.tensor_copy(aoT[:, vc, g * 512:(g + 1) * 512], ap),
                     reads=list(kk), writes=[("aoT" + sfx, vc, g)])

        for g in range(NG):
            for j in range(4):
                emit_scores_softmax(4 * g + j)
            for j in range(4):
                emit_T(4 * g + j)
            emit_AV(g)
        b.op("dve", lambda e: e.reciprocal(ri_all[:], rs_all[:]),
             reads=[("rs", t) for t in range(NT)], writes=[("ri",)])

        ar.free("qT" + sfx)
        ar.free("kT" + sfx)
        ar.free("v" + sfx)
        ar.free("sc" + sfx)
        ar.free("E" + sfx)
        ar.free("ET" + sfx)

        r1 = ar.alloc("r1" + sfx, [P, 2, D], F32)
        ybuf = ar.alloc("y" + sfx, [P, 4, D], F32)
        yT = ar.alloc("yT" + sfx, [P, ND, S], BF16)

        for t in range(NT):
            sl = t % 2
            bz = 0 if t % 2 == 0 else 2
            b.dma("sp", lambda e, t=t, sl=sl: e.dma_start(out=xs[:, sl, :], in_=x_d.ap()[s, t * P:(t + 1) * P, :]),
                  writes=[("xs", sl)], key=f"xs{sl}")
            for half in range(2):
                for vc in range(ND):
                    ap, kk = pf((bz + half) * 512, 512)
                    b.op("pe", lambda e, ap=ap, vc=vc, t=t, half=half:
                         e.matmul(ap, aoT[:, vc, t * P:(t + 1) * P], wob[:, vc, half * 512:(half + 1) * 512],
                                  start=(vc == 0), stop=(vc == ND - 1)),
                         reads=[("wob" + sfx,), ("aoT" + sfx, vc, t // 4)], writes=list(kk))
            apz, kkz = pf(bz * 512, 1024)
            b.op("dve", lambda e, apz=apz, sl=sl, t=t:
                 e.scalar_tensor_tensor(r1[:, sl, :], apz, ri_all[:, t:t + 1], xs[:, sl, :], ALU.mult, ALU.add),
                 reads=list(kkz) + [("xs", sl), ("ri",)], writes=[("r1" + sfx, sl)])
            sl4 = t % 4
            emit_ln(r1[:, sl, :], [("r1" + sfx, sl)], ybuf[:, sl4, :], [("y" + sfx, sl4)], sl)
            b.dma("sp", lambda e, t=t, sl4=sl4: e.dma_start(out=y_dram.ap()[t * P:(t + 1) * P, :], in_=ybuf[:, sl4, :]),
                  reads=[("y" + sfx, sl4)], writes=[("ydram", t)], key=f"yst{sl4}")
            b.dma("pool", lambda e, t=t, sl4=sl4: e.dma_start(out=ybf_dram.ap()[t * P:(t + 1) * P, :], in_=ybuf[:, sl4, :]),
                  reads=[("y" + sfx, sl4)], writes=[("ybf", t)], key=f"ybst{sl4}")
            if t % 8 == 7:
                half_seq = t // 8
                for dc in range(ND):
                    b.dma("sp", lambda e, dc=dc, hs=half_seq:
                          e.dma_start_transpose(yT[:, dc, hs * 1024:(hs + 1) * 1024],
                                                ybf_dram.ap()[hs * 1024:(hs + 1) * 1024, dc * P:(dc + 1) * P]),
                          reads=[("ybf", tt) for tt in range(half_seq * 8, half_seq * 8 + 8)],
                          writes=[("yT" + sfx, dc, half_seq)], key=f"yTt{half_seq}", coarse=True)

        ar.free("aoT" + sfx)
        ar.free("wob" + sfx)
        ar.free("r1" + sfx)
        ar.free("y" + sfx)

        h_actT = ar.alloc("h" + sfx, [P, NH, S], BF16)
        wgTb = ar.alloc("wgTb" + sfx, [P, 2, ND, 512], BF16)
        wuTb = ar.alloc("wuTb" + sfx, [P, 2, ND, 512], BF16)
        sg = ar.alloc("sg" + sfx, [P, 2, 512], BF16)

        for hgrp in range(NH // 4):
            slw = hgrp % 2
            for dc in range(ND):
                b.dma("sp", lambda e, dc=dc, hgrp=hgrp, slw=slw:
                      e.dma_start_transpose(wgTb[:, slw, dc, :],
                                            wgbf_d.ap()[hgrp * 512:(hgrp + 1) * 512, dc * P:(dc + 1) * P]),
                      reads=[("wgbf",)], writes=[("wgTb" + sfx, slw)], key=f"wgt{slw}", coarse=True)
                b.dma("sp", lambda e, dc=dc, hgrp=hgrp, slw=slw:
                      e.dma_start_transpose(wuTb[:, slw, dc, :],
                                            wubf_d.ap()[hgrp * 512:(hgrp + 1) * 512, dc * P:(dc + 1) * P]),
                      reads=[("wubf",)], writes=[("wuTb" + sfx, slw)], key=f"wut{slw}", coarse=True)
            for hl in range(4):
                hc = hgrp * 4 + hl
                for npair in range(2):
                    for dc in range(ND):
                        for i in (0, 1):
                            n = 2 * npair + i
                            apg, kkg = pf((i) * 512, 512)
                            b.op("pe", lambda e, apg=apg, dc=dc, n=n, slw=slw, hl=hl:
                                 e.matmul(apg, wgTb[:, slw, dc, hl * P:(hl + 1) * P],
                                          yT[:, dc, n * 512:(n + 1) * 512],
                                          start=(dc == 0), stop=(dc == ND - 1)),
                                 reads=[("wgTb" + sfx, slw), ("yT" + sfx, dc, n // 2)],
                                 writes=list(kkg))
                    for dc in range(ND):
                        for i in (0, 1):
                            n = 2 * npair + i
                            apu, kku = pf((2 + i) * 512, 512)
                            b.op("pe", lambda e, apu=apu, dc=dc, n=n, slw=slw, hl=hl:
                                 e.matmul(apu, wuTb[:, slw, dc, hl * P:(hl + 1) * P],
                                          yT[:, dc, n * 512:(n + 1) * 512],
                                          start=(dc == 0), stop=(dc == ND - 1)),
                                 reads=[("wuTb" + sfx, slw), ("yT" + sfx, dc, n // 2)],
                                 writes=list(kku))
                    for i in (0, 1):
                        n = 2 * npair + i
                        apg, kkg = pf(i * 512, 512)
                        apu, kku = pf((2 + i) * 512, 512)
                        b.op("act", lambda e, apg=apg, n=n: e.activation(sg[:, n % 2, :], apg, AF.Silu, scale=DEQ),
                             reads=list(kkg), writes=[("sg" + sfx, n % 2)])
                        b.op("dve", lambda e, apu=apu, n=n, hc=hc:
                             e.tensor_tensor(h_actT[:, hc, n * 512:(n + 1) * 512], apu, sg[:, n % 2, :], ALU.mult),
                             reads=list(kku) + [("sg" + sfx, n % 2)], writes=[("h" + sfx, hc, n)])

        ar.free("yT" + sfx)
        ar.free("wgTb" + sfx)
        ar.free("wuTb" + sfx)
        ar.free("sg" + sfx)

        wdT = ar.alloc("wdT" + sfx, [P, NH, D], BF16)
        ys = ar.alloc("ys" + sfx, [P, 2, D], F32)
        r2 = ar.alloc("r2" + sfx, [P, 2, D], F32)
        outb = ar.alloc("outb" + sfx, [P, 4, D], F32)

        for hc in range(NH):
            b.dma("sp", lambda e, hc=hc: e.dma_start_transpose(wdT[:, hc, :], wdbf_d.ap()[:, hc * P:(hc + 1) * P]),
                  reads=[("wdbf",)], writes=[("wdT" + sfx, hc)], key="wdt", coarse=True)

        for t in range(NT):
            sl = t % 2
            bz = 0 if t % 2 == 0 else 2
            b.dma("sp", lambda e, t=t, sl=sl: e.dma_start(out=ys[:, sl, :], in_=y_dram.ap()[t * P:(t + 1) * P, :]),
                  reads=[("ydram", t)], writes=[("ys" + sfx, sl)], key=f"ys{sl}")
            for hc in range(NH):
                for half in range(2):
                    ap, kk = pf((bz + half) * 512, 512)
                    b.op("pe", lambda e, ap=ap, hc=hc, t=t, half=half:
                         e.matmul(ap, h_actT[:, hc, t * P:(t + 1) * P], wdT[:, hc, half * 512:(half + 1) * 512],
                                  start=(hc == 0), stop=(hc == NH - 1)),
                         reads=[("h" + sfx, hc, t // 4), ("wdT" + sfx, hc)], writes=list(kk))
            apz, kkz = pf(bz * 512, 1024)
            b.op("dve", lambda e, apz=apz, sl=sl:
                 e.scalar_tensor_tensor(r2[:, sl, :], apz, DEQ * DEQ, ys[:, sl, :], ALU.mult, ALU.add),
                 reads=list(kkz) + [("ys" + sfx, sl)], writes=[("r2" + sfx, sl)])
            sl4 = t % 4
            emit_ln(r2[:, sl, :], [("r2" + sfx, sl)], outb[:, sl4, :], [("outb" + sfx, sl4)], sl)
            b.dma("sp", lambda e, t=t, sl4=sl4: e.dma_start(out=out_d.ap()[s, t * P:(t + 1) * P, :], in_=outb[:, sl4, :]),
                  reads=[("outb" + sfx, sl4)], key=f"ost{sl4}")

        ar.free("wdT" + sfx)
        ar.free("h" + sfx)
        ar.free("ys" + sfx)
        ar.free("r2" + sfx)
        ar.free("outb" + sfx)

    for s in range(SEQ_PER_CORE):
        emit_seq(s)

    stats = b.finalize_and_emit(final_wait_keys=["ost0", "ost1", "ost2", "ost3"])
    stats["sbuf_peak"] = ar.peak
    return nc, stats


def kernel(L0_attn_x, L0_attn_Wq, L0_attn_Wk, L0_attn_Wv, L0_attn_Wo,
           L0_ffn_w_gate, L0_ffn_w_up, L0_ffn_w_down):
    x = np.ascontiguousarray(np.asarray(L0_attn_x, dtype=np.float32))
    wq = np.ascontiguousarray(np.asarray(L0_attn_Wq, dtype=np.float32))
    wk = np.ascontiguousarray(np.asarray(L0_attn_Wk, dtype=np.float32))
    wv = np.ascontiguousarray(np.asarray(L0_attn_Wv, dtype=np.float32))
    wo = np.ascontiguousarray(np.asarray(L0_attn_Wo, dtype=np.float32))
    wg = np.ascontiguousarray(np.asarray(L0_ffn_w_gate, dtype=np.int32))
    wu = np.ascontiguousarray(np.asarray(L0_ffn_w_up, dtype=np.int32))
    wd = np.ascontiguousarray(np.asarray(L0_ffn_w_down, dtype=np.int32))

    nc, stats = build_program()
    in_maps = []
    for c in range(N_CORES):
        in_maps.append({
            "x": x[c * SEQ_PER_CORE:(c + 1) * SEQ_PER_CORE],
            "wq": wq, "wk": wk, "wv": wv, "wo": wo,
            "wg": wg, "wu": wu, "wd": wd,
        })
    res = run_bass_kernel_spmd(nc, in_maps, core_ids=list(range(N_CORES)))
    out = np.concatenate([r["out"] for r in res.results], axis=0)
    return out.astype(np.float32)


if __name__ == "__main__":
    nc, stats = build_program()
    print("build ok:", stats)
